# version 16
# baseline (speedup 1.0000x reference)
import contextlib
import numpy as np

import concourse.bacc as bacc
import concourse.mybir as mybir
from concourse.bass_utils import run_bass_kernel_spmd
from concourse.bass_core_rust import IndirectOffsetOnAxis

F32 = mybir.dt.float32
I16 = mybir.dt.int16
U32 = mybir.dt.uint32
AF = mybir.ActivationFunctionType
ALU = mybir.AluOpType
AX = mybir.AxisListType

B, S, D = 8, 4096, 1024
NCORES = 8
TL = S
NT = TL // 128
DC = D // 128
E = 8
FF = 4096
CAP = 5120
TG = B * S
NCOL = TG // 16
NCOLL = TL // 16
EPS = 1e-5
TRASH = CAP
FBLK = 2
FBS = FF // FBLK
FCB = FBS // 128
TSL = 256
NSL = CAP // TSL
NST = CAP // 128
NSC = 32

_STAGE = "full"


def _host_consts():
    ident = np.eye(128, dtype=np.float32)
    btriu = np.zeros((128, 128), np.float32)
    for e in range(8):
        for q2 in range(16):
            for q in range(16):
                if q2 < q:
                    btriu[16 * e + q2, 16 * e + q] = 1.0
    gspread = np.zeros((128, 128), np.float32)
    for m in range(128):
        gspread[16 * (m // 16) + 15, m] = 1.0
    gsum16 = np.zeros((128, 16), np.float32)
    gsumE = np.zeros((128, 16), np.float32)
    for e in range(8):
        for q in range(16):
            gsum16[16 * e + q, q] = 1.0
            gsumE[16 * e + q, q] = float(CAP * e)
    sel15 = np.zeros((128, 8), np.float32)
    for e in range(8):
        sel15[16 * e + 15, e] = 1.0
    grp8 = np.zeros((8, 128), np.float32)
    for e in range(8):
        grp8[e, 16 * e:16 * (e + 1)] = 1.0
    econst = (np.arange(128, dtype=np.float32) // 16).reshape(128, 1)
    gid = np.zeros((128, 256), np.float32)
    for i in range(TG // NCORES // 16 * 128 if False else 32768):
        gid[i % 128, i // 128] = float(i + 1)
    ctrash = np.zeros((128, 2048), np.float32)
    for p in range(128):
        q = p % 16
        for j in range(2048):
            ctrash[p, j] = float(CAP + ((16 * j + q) % 1024))
    onesr = np.ones((1, 128), np.float32)
    epsv = np.full((128, 1), EPS, np.float32)
    return dict(ident=ident, btriu=btriu, gspread=gspread, gsum16=gsum16,
                gsumE=gsumE, sel15=sel15, grp8=grp8, econst=econst,
                onesr=onesr, epsv=epsv, gid=gid, ctrash=ctrash,
                zeros=np.zeros((128, 1024), np.float32))


def _build(stage):
    nc = bacc.Bacc("TRN2", target_bir_lowering=False)
    dt = nc.dram_tensor

    x_in = dt("x", [TL, D], F32, kind="ExternalInput").ap()
    wq = dt("wq", [D, D], F32, kind="ExternalInput").ap()
    gwt = dt("gwt", [D, E], F32, kind="ExternalInput").ap()
    gb = dt("gb", [1, E], F32, kind="ExternalInput").ap()
    w1 = dt("w1", [D, FF], F32, kind="ExternalInput").ap()
    b1t = dt("b1t", [128, FF // 128], F32, kind="ExternalInput").ap()
    w2 = dt("w2", [FF, D], F32, kind="ExternalInput").ap()
    b2r = dt("b2r", [1, D], F32, kind="ExternalInput").ap()
    gsel = dt("gsel", [128, 16], F32, kind="ExternalInput").ap()
    corelt = dt("corelt", [8, 1], F32, kind="ExternalInput").ap()
    C = {k: dt(k, list(v.shape), F32, kind="ExternalInput").ap()
         for k, v in _host_consts().items()}
    out = dt("out", [TL, D], F32, kind="ExternalOutput").ap()

    x2d = dt("x2d", [TL, D], F32, kind="Internal").ap()
    h2shard = dt("h2shard", [TL + 5, D], F32, kind="Internal").ap()
    h2glob = dt("h2glob", [TG, D], F32, kind="Internal").ap()
    idxg2 = dt("idxg2", [NCORES * 5, D], F32, kind="Internal").ap()
    cnt8 = dt("cnt8", [8, 1], F32, kind="Internal").ap()
    cnttab = dt("cnttab", [8, 8], F32, kind="Internal").ap()
    idtab = dt("idtab", [8 * (CAP + 1024), 64], F32, kind="Internal").ap()
    slabf = dt("slabf", [D, CAP], F32, kind="Internal").ap()
    y2partial = dt("y2partial", [CAP, D], F32, kind="Internal").ap()
    y2slab = dt("y2slab", [CAP, D], F32, kind="Internal").ap()
    y2glob = dt("y2glob", [NCORES * CAP, D], F32, kind="Internal").ap()

    es = contextlib.ExitStack()
    with es:
        sbt = lambda name, shape, dtt=F32: es.enter_context(
            nc.sbuf_tensor(name, shape, dtt))
        sem = lambda name: es.enter_context(nc.semaphore(name))
        pb = [es.enter_context(nc.psum_tensor(f"pb{k}", [128, 512], F32))
              for k in range(8)]

        cident = sbt("cident", [128, 128])
        cbtriu = sbt("cbtriu", [128, 128])
        cgspread = sbt("cgspread", [128, 128])
        cgsum16 = sbt("cgsum16", [128, 16])
        cgsumE = sbt("cgsumE", [128, 16])
        csel15 = sbt("csel15", [128, 8])
        cgrp8 = sbt("cgrp8", [8, 128])
        cecon = sbt("cecon", [128, 1])
        conesr = sbt("conesr", [1, 128])
        cepsv = sbt("cepsv", [128, 1])
        cgsel = sbt("cgsel", [128, 16])
        idcols = sbt("idcols", [128, NST], U32)
        ccorelt = sbt("ccorelt", [8, 1])
        sgwt = sbt("sgwt", [128, DC * E])
        sgb = sbt("sgb", [1, E])
        sb2r = sbt("sb2r", [1, D])
        sb1t = sbt("sb1t", [128, FF // 128])
        mcol = sbt("mcol", [128, NT])
        rcol = sbt("rcol", [128, NT])
        ncol = sbt("ncol", [128, NT])
        tcol = sbt("tcol", [128, NT])
        idxl = sbt("idxl", [128, NT])
        tpl = sbt("tpl", [128, NT])
        il16 = sbt("il16", [16, 320])
        tp16 = sbt("tp16", [16, NCOLL])
        g16 = sbt("g16", [16, NCOLL])
        s16 = sbt("s16", [16, NCOLL])
        gcols = sbt("gcols", [128, 32], U32)
        maskl = sbt("maskl", [128, NCOLL])
        colpl = sbt("colpl", [128, NCOLL])
        rexl = sbt("rexl", [128, NCOLL])
        cnts = sbt("cnts", [8, 1])
        scols = sbt("scols", [128, 32])

        cc = sem("cc")
        tn = sem("tn"); bar = sem("bar")
        vst = sem("vst"); ssq = sem("ssq"); vrs = sem("vrs"); sh1 = sem("sh1")
        evA = sem("evA"); evH = sem("evH"); evF = sem("evF")
        vX2 = sem("vX2"); vSm = sem("vSm"); sEx = sem("sEx")
        v5 = sem("v5"); t5 = sem("t5")
        evB0 = sem("evB0"); evB1 = sem("evB1")
        evC = sem("evC"); evD = sem("evD"); vO = sem("vO")

        class Q:
            def __init__(self, name, eng):
                self.h = sem(name)
                self.n = 0
                self.eng = eng
        QREG = []
        def mkq(name, eng):
            q = Q(name, eng); QREG.append(q); return q
        def dma(engapi, q, out_ap, in_ap):
            engapi.dma_start(out=out_ap, in_=in_ap).then_inc(q.h, 16)
            q.n += 1
            return q.n
        ldc = mkq("ldc", "sync")
        ldx = [mkq(f"ldx{k}", "sync") for k in range(2)]
        ldw = [mkq(f"ldw{k}", "sync") for k in range(2)]
        ldr = mkq("ldr", "sync")
        ldz = mkq("ldz", "sync")
        lds = [mkq(f"lds{k}", "sync") for k in range(2)]
        lsl = [mkq(f"lsl{k}", "sync") for k in range(2)]
        ldwb = mkq("ldwb", "sync")
        lsf = [mkq(f"lsf{k}", "sync") for k in range(2)]
        lyp = [mkq(f"lyp{k}", "sync") for k in range(2)]
        lx2 = [mkq(f"lx2{k}", "sync") for k in range(2)]
        stx = mkq("stx", "scalar")
        smh = mkq("smh", "scalar")
        stf = [mkq(f"stf{k}", "scalar") for k in range(2)]
        sty = [mkq(f"sty{k}", "scalar") for k in range(2)]
        sid = mkq("sid", "scalar")
        scn = mkq("scn", "scalar")
        sto = mkq("sto", "scalar")
        sop = [mkq(f"sop{k}", "scalar") for k in range(2)]
        qsc = [mkq(f"qsc{k}", "gpsimd") for k in range(8)]
        qid = [mkq(f"qid{k}", "sync") for k in range(2)]
        qpz = mkq("qpz", "scalar")
        qsl = [mkq(f"qsl{k}", "gpsimd") for k in range(2)]
        qg4 = [mkq(f"qg4_{k}", "gpsimd") for k in range(4)]
        lx4 = [mkq(f"lx4_{k}", "sync") for k in range(4)]
        sp4 = [mkq(f"sp4_{k}", "scalar") for k in range(4)]
        qgi = [mkq(f"qgi{k}", "gpsimd") for k in range(2)]

        ct = {"cc": 0, "tn": 0, "t5": 0, "v5": 0}
        NBAR = [0]
        blk = es.enter_context(nc.Block())

        def barrier():
            NBAR[0] += 1
            tgt = 5 * NBAR[0]
            snap = [(q, q.n) for q in QREG if q.n > 0]

            @blk.sync
            def _(s):
                for q, n in snap:
                    if q.eng == "sync":
                        s.wait_ge(q.h, 16 * n)
                s.sem_inc(bar, 1)
                s.wait_ge(bar, tgt)

            @blk.tensor
            def _(t):
                t.drain()
                t.sem_inc(bar, 1)
                t.wait_ge(bar, tgt)

            @blk.vector
            def _(v):
                v.drain()
                v.sem_inc(bar, 1)
                v.wait_ge(bar, tgt)

            @blk.scalar
            def _(s):
                s.drain()
                for q, n in snap:
                    if q.eng == "scalar":
                        s.wait_ge(q.h, 16 * n)
                s.sem_inc(bar, 1)
                s.wait_ge(bar, tgt)

            @blk.gpsimd
            def _(g):
                for q, n in snap:
                    if q.eng == "gpsimd":
                        g.wait_ge(q.h, 16 * n)
                g.sem_inc(bar, 1)
                g.wait_ge(bar, tgt)

        @blk.sync
        def _(s):
            for apto, apfrom in [
                (cident, C["ident"]), (cbtriu, C["btriu"]),
                (cgspread, C["gspread"]), (cgsum16, C["gsum16"]),
                (cgsumE, C["gsumE"]), (csel15, C["sel15"]),
                (cgrp8, C["grp8"]), (cecon, C["econst"]),
                (conesr, C["onesr"]), (cepsv, C["epsv"]),
                (cgsel, gsel), (ccorelt, corelt), (sgb, gb),
                (sb2r, b2r), (sb1t, b1t),
            ]:
                dma(s, ldc, apto[:, :], apfrom)
            for k in range(DC):
                dma(s, ldc, sgwt[:, k * E:(k + 1) * E],
                    gwt[k * 128:(k + 1) * 128, :])
        flatz = idtab.rearrange("r c -> (r c)")
        @blk.scalar
        def _(s):
            for k in range(8):
                base = k * (CAP + 1024) * 64
                for n in range(3):
                    dma(s, qpz,
                        flatz[base + n * 131072:base + (n + 1) * 131072]
                        .rearrange("(p c) -> p c", p=128),
                        C["zeros"])
        LDC_T = 16 * ldc.n

        regA = contextlib.ExitStack()
        es.enter_context(regA)
        ra = lambda name, shape, dtt=F32: regA.enter_context(
            nc.sbuf_tensor(name, shape, dtt))
        h1 = ra("h1", [128, NT * D])
        MG = ra("MG", [128, DC * D])
        xt = [ra(f"xt{k}", [128, D]) for k in range(2)]
        x2t = ra("x2t", [128, D])
        h2t = ra("h2t", [128, D])
        sqt = ra("sqt", [128, D])
        h1T = [ra(f"h1T{k}", [128, D]) for k in range(2)]
        h2T = ra("h2T", [128, D])
        wqb = [sqt[:, k * 128:(k + 1) * 128] for k in range(2)]
        lgt = ra("lgt", [128, 16])
        aidx = ra("aidx", [128, 8], U32)

        @blk.sync
        def _(s):
            for i in range(NT):
                if i >= 2:
                    s.wait_ge(sh1, i - 1)
                dma(s, ldx[i % 2], xt[i % 2][:, :],
                    x_in[i * 128:(i + 1) * 128, :])

        @blk.vector
        def _(v):
            for i in range(NT):
                v.wait_ge(ldx[i % 2].h, 16 * (i // 2 + 1))
                v.reduce_sum(tcol[:, i:i + 1], xt[i % 2][:, :], axis=AX.X)
                v.tensor_mul(sqt[:, :], xt[i % 2][:, :], xt[i % 2][:, :])
                v.drain()
                v.reduce_sum(mcol[:, i:i + 1], sqt[:, :], axis=AX.X)
                v.drain()
                v.tensor_scalar_mul(ncol[:, i:i + 1], tcol[:, i:i + 1], 1.0 / D)
                v.drain()
                v.tensor_scalar_mul(tcol[:, i:i + 1], mcol[:, i:i + 1], 1.0 / D)
                v.drain()
                v.tensor_mul(mcol[:, i:i + 1], ncol[:, i:i + 1], ncol[:, i:i + 1])
                v.drain()
                v.tensor_sub(tcol[:, i:i + 1], tcol[:, i:i + 1], mcol[:, i:i + 1])
                v.drain()
                v.tensor_copy(mcol[:, i:i + 1], ncol[:, i:i + 1]).then_inc(vst, 1)
                v.wait_ge(ssq, i + 1)
                v.reciprocal(rcol[:, i:i + 1], tcol[:, i:i + 1])
                v.drain()
                v.tensor_mul(ncol[:, i:i + 1], mcol[:, i:i + 1], rcol[:, i:i + 1])
                v.drain()
                v.tensor_scalar_mul(ncol[:, i:i + 1], ncol[:, i:i + 1], -1.0).then_inc(vrs, 1)

        @blk.scalar
        def _(s):
            s.wait_ge(ldc.h, LDC_T)
            for i in range(NT):
                s.wait_ge(vst, i + 1)
                s.activation(tcol[:, i:i + 1], tcol[:, i:i + 1], AF.Sqrt,
                             bias=cepsv[:, :], scale=1.0).then_inc(ssq, 1)
                s.wait_ge(vrs, i + 1)
                s.activation(h1[:, i * D:(i + 1) * D], xt[i % 2][:, :], AF.Identity,
                             bias=ncol[:, i:i + 1], scale=rcol[:, i:i + 1]).then_inc(sh1, 1)

        @blk.tensor
        def _(t):
            for half in range(2):
                if half == 1:
                    t.wait_ge(v5, ct["v5"] + 1)
                for i in range(NT):
                    if half == 0:
                        t.wait_ge(sh1, i + 1)
                    for d1 in range(DC):
                        ins = t.matmul(pb[d1][:, :],
                                 h1[:, i * D + d1 * 128:i * D + (d1 + 1) * 128],
                                 h1[:, i * D + half * 512:i * D + half * 512 + 512],
                                 start=(i == 0), stop=(i == NT - 1))
                ins.then_inc(tn, 1)
                ct["tn"] += 1
        TN_GRAM0, TN_GRAM1 = ct["tn"] - 1, ct["tn"]

        @blk.vector
        def _(v):
            for half in range(2):
                v.wait_ge(tn, TN_GRAM0 + half)
                for d1 in range(DC):
                    ins = v.tensor_copy(
                        MG[:, d1 * D + half * 512:d1 * D + half * 512 + 512],
                        pb[d1][:, :])
                ins.then_inc(v5, 1)
                ct["v5"] += 1
        V5_GRAM = ct["v5"]

        @blk.sync
        def _(s):
            s.wait_ge(sh1, NT)
            n = 0
            for half in range(2):
                for dout in range(DC):
                    for k in range(DC):
                        if n >= 2:
                            s.wait_ge(tn, TN_GRAM1 + n - 1)
                        dma(s, ldw[n % 2], wqb[n % 2][:, :],
                            wq[k * 128:(k + 1) * 128, dout * 128:(dout + 1) * 128])
                        n += 1

        @blk.tensor
        def _(t):
            n = 0
            for half in range(2):
                t.wait_ge(v5, V5_GRAM - 1 + half + 1)
                for dout in range(DC):
                    for k in range(DC):
                        t.wait_ge(ldw[n % 2].h, 16 * (n // 2 + 1))
                        t.matmul(pb[dout][:, :], wqb[n % 2][:, :],
                                 MG[:, k * D + half * 512:k * D + half * 512 + 512],
                                 start=(k == 0), stop=(k == DC - 1)).then_inc(tn, 1)
                        ct["tn"] += 1
                        n += 1
        TN_P3 = ct["tn"]

        @blk.vector
        def _(v):
            for half in range(2):
                v.wait_ge(tn, TN_GRAM1 + 64 * (half + 1))
                for dout in range(DC):
                    ins = v.tensor_copy(
                        MG[:, dout * D + half * 512:dout * D + half * 512 + 512],
                        pb[dout][:, :])
                ins.then_inc(v5, 1)
                ct["v5"] += 1
        V5_P3 = ct["v5"]

        barrier()

        @blk.sync
        def _(s):
            for i in range(NT):
                if i >= 2:
                    s.wait_ge(vX2, i - 1)
                dma(s, ldx[i % 2], xt[i % 2][:, :],
                    x_in[i * 128:(i + 1) * 128, :])

        TN_P4 = ct["tn"]
        NSTO = 3 if stage == "attn" else 2

        p_tr = {}; p_attn = {}; p_h2t = {}; p_gate = {}
        _c = TN_P4
        p_tr[0] = _c; _c += 8
        for i in range(NT):
            p_attn[i] = _c; _c += 2
            if i + 1 < NT:
                p_tr[i + 1] = _c; _c += 8
            if i >= 1:
                p_h2t[i - 1] = _c; _c += 8
                p_gate[i - 1] = _c; _c += 1
        p_h2t[NT - 1] = _c; _c += 8
        p_gate[NT - 1] = _c; _c += 1
        TN_P4_END = _c

        def emit_tr(t, i):
            for d in range(DC):
                k = 8 * i + d
                if k >= 2:
                    t.wait_ge(evA, k - 1)
                t.transpose(pb[6 + (k % 2)][:, 0:128],
                            h1[:, i * D + d * 128:i * D + (d + 1) * 128],
                            cident[:, :]).then_inc(tn, 1)

        def emit_attn(t, i):
            t.wait_ge(evA, 8 * (i + 1))
            if i >= 1:
                t.wait_ge(vX2, i)
            for half in range(2):
                for d1 in range(DC):
                    ins = t.matmul(pb[half][:, :],
                             h1T[i % 2][:, d1 * 128:(d1 + 1) * 128],
                             MG[:, d1 * D + half * 512:d1 * D + half * 512 + 512],
                             start=(d1 == 0), stop=(d1 == DC - 1))
                ins.then_inc(tn, 1)

        def emit_h2t(t, i):
            t.wait_ge(sh1, NT + i + 1)
            for d in range(DC):
                k = 8 * i + d
                if k >= 2:
                    t.wait_ge(evH, k - 1)
                t.transpose(pb[3 + (k % 2)][:, 0:128],
                            h2t[:, d * 128:(d + 1) * 128],
                            cident[:, :]).then_inc(tn, 1)

        def emit_gate(t, i):
            t.wait_ge(evH, 8 * (i + 1))
            if i >= 1:
                t.wait_ge(vSm, i)
            for d in range(DC):
                t.matmul(pb[2][:, 0:E], h2T[:, d * 128:(d + 1) * 128],
                         sgwt[:, d * E:(d + 1) * E],
                         start=(d == 0), stop=False)
            t.matmul(pb[2][:, 0:E], conesr[:, :], sgb[:, :],
                     start=False, stop=True).then_inc(tn, 1)

        @blk.tensor
        def _(t):
            t.wait_ge(ldc.h, LDC_T)
            emit_tr(t, 0)
            for i in range(NT):
                emit_attn(t, i)
                if i + 1 < NT:
                    emit_tr(t, i + 1)
                if i >= 1:
                    emit_h2t(t, i - 1)
                    emit_gate(t, i - 1)
            emit_h2t(t, NT - 1)
            emit_gate(t, NT - 1)
        ct["tn"] = TN_P4_END

        @blk.vector
        def _(v):
            def evac_tr(v, i):
                for d in range(DC):
                    v.wait_ge(tn, p_tr[i] + d + 1)
                    if i >= 2 and d == 0:
                        v.wait_ge(tn, p_attn[i - 2] + 2)
                    v.tensor_copy(h1T[i % 2][:, d * 128:(d + 1) * 128],
                                  pb[6 + ((8 * i + d) % 2)][:, 0:128]).then_inc(evA, 1)

            def gate_soft(v, i):
                v.wait_ge(tn, p_gate[i] + 1)
                v.tensor_copy(lgt[:, 0:E], pb[2][:, 0:E])
                v.drain()
                v.max(lgt[:, 8:16], lgt[:, 0:E])
                v.drain()
                v.max_index(aidx[:, 0:8], lgt[:, 8:16], lgt[:, 0:E])
                v.drain()
                v.tensor_copy(idxl[:, i:i + 1], aidx[:, 0:1])
                v.tensor_scalar_mul(tcol[:, i:i + 1], lgt[:, 8:9], -1.0)
                v.drain()
                v.sem_inc(vSm, 1)
                v.wait_ge(sEx, i + 1)
                v.reduce_sum(tpl[:, i:i + 1], lgt[:, 0:E], axis=AX.X)
                v.drain()
                v.reciprocal(tpl[:, i:i + 1], tpl[:, i:i + 1])
                v.drain()

            evac_tr(v, 0)
            for i in range(NT):
                v.wait_ge(ldx[i % 2].h, 16 * (NT // 2 + i // 2 + 1))
                if i >= 1:
                    v.wait_ge(stx.h, 16 * i)
                for half in range(2):
                    v.wait_ge(tn, p_attn[i] + half + 1)
                    v.tensor_add(x2t[:, half * 512:half * 512 + 512],
                                 xt[i % 2][:, half * 512:half * 512 + 512],
                                 pb[half][:, :])
                v.drain()
                v.sem_inc(vX2, 1)
                if i + 1 < NT:
                    evac_tr(v, i + 1)
                v.reduce_sum(tcol[:, i:i + 1], x2t[:, :], axis=AX.X)
                v.tensor_mul(sqt[:, :], x2t[:, :], x2t[:, :])
                v.drain()
                v.reduce_sum(mcol[:, i:i + 1], sqt[:, :], axis=AX.X)
                v.drain()
                v.tensor_scalar_mul(ncol[:, i:i + 1], tcol[:, i:i + 1], 1.0 / D)
                v.drain()
                v.tensor_scalar_mul(tcol[:, i:i + 1], mcol[:, i:i + 1], 1.0 / D)
                v.drain()
                v.tensor_mul(mcol[:, i:i + 1], ncol[:, i:i + 1], ncol[:, i:i + 1])
                v.drain()
                v.tensor_sub(tcol[:, i:i + 1], tcol[:, i:i + 1], mcol[:, i:i + 1])
                v.drain()
                v.tensor_copy(mcol[:, i:i + 1], ncol[:, i:i + 1]).then_inc(vst, 1)
                v.wait_ge(ssq, NT + i + 1)
                v.reciprocal(rcol[:, i:i + 1], tcol[:, i:i + 1])
                v.drain()
                v.tensor_mul(ncol[:, i:i + 1], mcol[:, i:i + 1], rcol[:, i:i + 1])
                v.drain()
                v.tensor_scalar_mul(ncol[:, i:i + 1], ncol[:, i:i + 1], -1.0).then_inc(vrs, 1)
                if i >= 1:
                    gate_soft(v, i - 1)
            gate_soft(v, NT - 1)

        @blk.scalar
        def _(s):
            def h2evac(s, i):
                for d in range(DC):
                    s.wait_ge(tn, p_h2t[i] + d + 1)
                    s.activation(h2T[:, d * 128:(d + 1) * 128],
                                 pb[3 + ((8 * i + d) % 2)][:, 0:128], AF.Copy,
                                 bias=0.0, scale=1.0).then_inc(evH, 1)

            def expx(s, i):
                s.wait_ge(vSm, i + 1)
                s.activation(lgt[:, 0:E], lgt[:, 0:E], AF.Exp,
                             bias=tcol[:, i:i + 1], scale=1.0)
                s.drain()
                s.sem_inc(sEx, 1)

            for i in range(NT):
                s.wait_ge(vst, NT + i + 1)
                s.activation(tcol[:, i:i + 1], tcol[:, i:i + 1], AF.Sqrt,
                             bias=cepsv[:, :], scale=1.0).then_inc(ssq, 1)
                if i >= 1:
                    h2evac(s, i - 1)
                    expx(s, i - 1)
                s.wait_ge(vrs, NT + i + 1)
                if i >= 1:
                    s.wait_ge(tn, p_h2t[i - 1] + 8)
                    s.wait_ge(smh.h, 16 * i)
                s.activation(h2t[:, :], x2t[:, :], AF.Identity,
                             bias=ncol[:, i:i + 1], scale=rcol[:, i:i + 1]).then_inc(sh1, 1)
                s.drain()
                dma(s, stx, x2d[i * 128:(i + 1) * 128, :], x2t[:, :])
                if stage == "attn":
                    dma(s, sto, out[i * 128:(i + 1) * 128, :], x2t[:, :])
                dma(s, smh, h2shard[i * 128:(i + 1) * 128, :], h2t[:, :])
            h2evac(s, NT - 1)
            expx(s, NT - 1)


        if stage == "attn":
            @blk.scalar
            def _(s):
                s.wait_ge(sto.h, 16 * sto.n)
            return nc

        barrier()

        TN_F = ct["tn"]

        @blk.tensor
        def _(t):
            for j in range(16):
                if j >= 2:
                    t.wait_ge(evF, j - 1)
                src = idxl if j < 8 else tpl
                ph = j % 8
                t.matmul(pb[5 + (j % 2)][0:16, 0:NT],
                         cident[:, 16 * ph:16 * (ph + 1)], src[:, :],
                         start=True, stop=True).then_inc(tn, 1)
        ct["tn"] += 16

        @blk.vector
        def _(v):
            for j in range(16):
                dst = il16 if j < 8 else tp16
                ph = j % 8
                v.wait_ge(tn, TN_F + j + 1)
                v.tensor_copy(dst[0:16, ph:NCOLL:8],
                              pb[5 + (j % 2)][0:16, 0:NT]).then_inc(evF, 1)

        irepl = sqt[:, 0:NCOLL]
        onesl = sqt[:, NCOLL:2 * NCOLL]
        colsuml = sqt[:, 2 * NCOLL:3 * NCOLL]
        cumsl = sqt[:, 3 * NCOLL:4 * NCOLL]

        @blk.sync
        def _(s):
            s.wait_ge(evF, 8)
            for e in range(8):
                dma(s, ldr, irepl[16 * e:16 * (e + 1), :], il16[:, 0:NCOLL])
        LDR_IREPL = 16 * ldr.n

        @blk.vector
        def _(v):
            v.memset(onesl[:, :], 1.0)
            v.memset(il16[:, NCOLL:320], 0.0)
            v.wait_ge(ldr.h, LDR_IREPL)
            v.wait_ge(ldc.h, LDC_T)
            v.tensor_single_scalar(maskl[:, :], irepl[:, :], cecon[:, :], ALU.is_equal)
            v.drain()
            v.sem_inc(v5, 1)
            ct["v5"] += 1
        V5_MASKL = ct["v5"]

        @blk.tensor
        def _(t):
            t.wait_ge(v5, V5_MASKL)
            t.matmul(pb[4][0:128, 0:NCOLL], cbtriu[:, :], maskl[:, :],
                     start=True, stop=True).then_inc(t5, 1)
            ct["t5"] += 1
        T5_COLPL = ct["t5"]

        @blk.vector
        def _(v):
            v.wait_ge(t5, T5_COLPL)
            v.tensor_copy(colpl[:, :], pb[4][0:128, 0:NCOLL])
            v.drain()
            v.tensor_add(colsuml[:, :], colpl[:, :], maskl[:, :])
            v.drain()
            v.tensor_tensor_scan(cumsl[:, :], onesl[:, :], colsuml[:, :], 0.0,
                                 ALU.mult, ALU.add)
            v.drain()
            v.tensor_sub(rexl[:, :], cumsl[:, :], colsuml[:, :])
            v.drain()
            v.sem_inc(v5, 1)
            ct["v5"] += 1
        V5_SCANL = ct["v5"]

        @blk.tensor
        def _(t):
            t.wait_ge(v5, V5_SCANL)
            t.matmul(pb[4][0:128, 0:NCOLL], cgspread[:, :], rexl[:, :],
                     start=True, stop=True)
            t.matmul(pb[5][0:8, 0:1], csel15[:, :], cumsl[:, NCOLL - 1:NCOLL],
                     start=True, stop=True).then_inc(t5, 1)
            ct["t5"] += 1
        T5_SPREADL = ct["t5"]

        @blk.vector
        def _(v):
            v.wait_ge(t5, T5_SPREADL)
            v.tensor_copy(rexl[:, :], pb[4][0:128, 0:NCOLL])
            v.tensor_copy(cnts[:, :], pb[5][0:8, 0:1])
            v.drain()
            v.tensor_copy(il16[0:8, NCOLL:NCOLL + 1], cnts[:, :])
            v.drain()
            v.sem_inc(v5, 1)
            ct["v5"] += 1
        V5_CNT = ct["v5"]

        @blk.scalar
        def _(s):
            s.wait_ge(v5, V5_CNT)
            dma(s, sid,
                h2shard[TL:TL + 5, :].rearrange("a b -> (a b)")
                .rearrange("(q s) -> q s", s=320),
                il16[:, :])

        @blk.gpsimd
        def _(g):
            g.wait_ge(sid.h, 16 * sid.n)
            g.collective_compute(
                "AllGather", ALU.bypass,
                replica_groups=[list(range(NCORES))],
                ins=[h2shard[TL:TL + 5, :].opt()],
                outs=[idxg2.opt()],
            ).then_inc(cc, 1)
            g.wait_ge(smh.h, 16 * 32)
            g.collective_compute(
                "AllGather", ALU.bypass,
                replica_groups=[list(range(NCORES))],
                ins=[h2shard[0:TL, :].opt()],
                outs=[h2glob[:, :].opt()],
            ).then_inc(cc, 1)
        ct["cc"] += 2

        barrier()
        regA.close()

        if stage == "p4b":
            @blk.scalar
            def _(s):
                dma(s, sto, out[0:16, 0:NCOLL], il16[:, :])
                dma(s, sto, out[16:32, 0:NCOLL], tp16[:, :])
                s.wait_ge(sto.h, 16 * sto.n)
            return nc

        regB = contextlib.ExitStack()
        es.enter_context(regB)
        rbt = lambda name, shape, dtt=F32: regB.enter_context(
            nc.sbuf_tensor(name, shape, dtt))
        irep = rbt("irep", [128, NCOL])
        maskg = rbt("maskg", [128, NCOL])
        colp = rbt("colp", [128, NCOL])
        colsum = rbt("colsum", [128, NCOL])
        cums = rbt("cums", [128, NCOL])
        rex = rbt("rex", [128, NCOL])
        grank = rbt("grank", [128, NCOL])
        onesw = rbt("onesw", [128, NCOL])
        fullsel = rbt("fullsel", [128, NCOL])
        i16f = rbt("i16f", [16, NCOL], I16)
        i16rep = rbt("i16rep", [128, NCOL], I16)
        ctrashr = rbt("ctrashr", [128, NCOL])
        idw = rbt("idw", [128, NST * 64])
        idwb = [rbt(f"idwb{k}", [128, NST * 64]) for k in range(2)]
        cgid = rbt("cgid", [128, 256])
        grankl = rbt("grankl", [128, NCOLL])
        kml = rbt("kml", [128, NCOLL])
        pcb = rbt("pcb", [128, 1])
        ctab = rbt("ctab", [8, 8])
        t16a = rbt("t16a", [16, NCOLL])
        t16a2 = rbt("t16a2", [16, NCOLL])
        t16b = rbt("t16b", [16, NCOLL])
        zt = rbt("zt", [128, D])

        CC_IDX = 1
        CC_H2 = 2

        idxt3 = (idxg2.rearrange("a b -> (a b)")
                 .rearrange("(c z) -> c z", c=NCORES)
                 .rearrange("c (q s) -> c q s", s=320))

        V5_0 = ct["v5"]

        @blk.sync
        def _(s):
            dma(s, ldr, ctrashr[:, :], C["ctrash"])
            dma(s, ldr, cgid[:, :], C["gid"])
            s.wait_ge(cc, CC_IDX)
            for e in range(8):
                dma(s, ldr,
                    irep[16 * e:16 * (e + 1), :].rearrange("q (c s) -> q c s", c=8),
                    idxt3.rearrange("c q s -> q c s")[:, :, 0:NCOLL])
            s.dma_start(
                out=ctab[:, :],
                in_=idxt3[:, 0:8, NCOLL:NCOLL + 1].rearrange("c q s -> c (q s)"),
                allow_slow_non_contiguous=True).then_inc(ldr.h, 16)
            ldr.n += 1
        LDR_IREP = 16 * ldr.n

        @blk.vector
        def _(v):
            v.memset(onesw[:, :], 1.0)
            v.memset(zt[:, :], 0.0)
            v.drain()
            v.sem_inc(v5, 1)
            ct["v5"] += 1
            v.wait_ge(ldr.h, LDR_IREP)
            v.tensor_single_scalar(maskg[:, :], irep[:, :], cecon[:, :], ALU.is_equal)
            v.drain()
            v.sem_inc(v5, 1)
            ct["v5"] += 1
        V5_MASK = ct["v5"]

        @blk.tensor
        def _(t):
            t.wait_ge(v5, V5_MASK)
            for j in range(3):
                t.matmul(pb[j][:, :], cbtriu[:, :], maskg[:, j * 512:(j + 1) * 512],
                         start=True, stop=True)
            t.matmul(pb[3][:, :], cbtriu[:, :], maskg[:, 3 * 512:4 * 512],
                     start=True, stop=True).then_inc(t5, 1)
            ct["t5"] += 1
        T5_COLP = ct["t5"]

        @blk.vector
        def _(v):
            v.wait_ge(t5, T5_COLP)
            for j in range(4):
                v.tensor_copy(colp[:, j * 512:(j + 1) * 512], pb[j][:, :])
            v.drain()
            v.tensor_add(colsum[:, :], colp[:, :], maskg[:, :])
            v.drain()
            v.tensor_tensor_scan(cums[:, :], onesw[:, :], colsum[:, :], 0.0,
                                 ALU.mult, ALU.add)
            v.drain()
            v.tensor_sub(rex[:, :], cums[:, :], colsum[:, :])
            v.drain()
            v.sem_inc(v5, 1)
            ct["v5"] += 1
        V5_SCAN = ct["v5"]

        @blk.tensor
        def _(t):
            t.wait_ge(v5, V5_SCAN)
            for j in range(3):
                t.matmul(pb[j][:, :], cgspread[:, :], rex[:, j * 512:(j + 1) * 512],
                         start=True, stop=True)
            t.matmul(pb[3][:, :], cgspread[:, :], rex[:, 3 * 512:4 * 512],
                     start=True, stop=True).then_inc(t5, 1)
            ct["t5"] += 1
        T5_SPREAD = ct["t5"]

        @blk.vector
        def _(v):
            v.wait_ge(t5, T5_SPREAD)
            for j in range(4):
                v.tensor_copy(rex[:, j * 512:(j + 1) * 512], pb[j][:, :])
            v.drain()
            v.tensor_add(grank[:, :], colp[:, :], rex[:, :])
            v.drain()
            v.tensor_single_scalar(fullsel[:, :], grank[:, :], float(CAP), ALU.is_lt)
            v.drain()
            v.tensor_mul(fullsel[:, :], fullsel[:, :], maskg[:, :])
            v.drain()
            v.tensor_sub(cums[:, :], grank[:, :], ctrashr[:, :])
            v.drain()
            v.tensor_mul(fullsel[:, :], fullsel[:, :], cums[:, :])
            v.drain()
            v.tensor_add(fullsel[:, :], fullsel[:, :], ctrashr[:, :])
            v.drain()
            v.sem_inc(v5, 1)
            ct["v5"] += 1
        V5_FSEL = ct["v5"]

        @blk.tensor
        def _(t):
            t.wait_ge(v5, V5_FSEL)
            for j in range(3):
                t.matmul(pb[j][0:16, :], cgsel[:, :], fullsel[:, j * 512:(j + 1) * 512],
                         start=True, stop=True)
            t.matmul(pb[3][0:16, :], cgsel[:, :], fullsel[:, 3 * 512:4 * 512],
                     start=True, stop=True).then_inc(t5, 1)
            ct["t5"] += 1
        T5_GSEL = ct["t5"]

        @blk.vector
        def _(v):
            v.wait_ge(t5, T5_GSEL)
            for j in range(4):
                v.tensor_copy(i16f[:, j * 512:(j + 1) * 512], pb[j][0:16, :])
            v.drain()
            v.sem_inc(v5, 1)
            ct["v5"] += 1
        V5_I16F = ct["v5"]

        @blk.tensor
        def _(t):
            t.wait_ge(ldr.h, LDR_IREP)
            t.matmul(pb[5][0:8, 0:1], ctab[:, :], ccorelt[:, :],
                     start=True, stop=True).then_inc(t5, 1)
            ct["t5"] += 1
        T5_PC1 = ct["t5"]

        @blk.vector
        def _(v):
            v.wait_ge(t5, T5_PC1)
            v.tensor_copy(cnts[:, :], pb[5][0:8, 0:1])
            v.drain()
            v.sem_inc(v5, 1)
            ct["v5"] += 1
        V5_PC1 = ct["v5"]

        @blk.tensor
        def _(t):
            t.wait_ge(v5, V5_PC1)
            t.matmul(pb[5][:, 0:1], cgrp8[:, :], cnts[:, :],
                     start=True, stop=True).then_inc(t5, 1)
            ct["t5"] += 1
        T5_PC2 = ct["t5"]

        @blk.vector
        def _(v):
            v.wait_ge(t5, T5_PC2)
            v.tensor_copy(pcb[:, :], pb[5][:, 0:1])
            v.drain()
            v.tensor_add(grankl[:, :], colpl[:, :], rexl[:, :])
            v.drain()
            v.tensor_scalar_add(grankl[:, :], grankl[:, :], pcb[:, :])
            v.drain()
            v.tensor_single_scalar(kml[:, :], grankl[:, :], float(CAP), ALU.is_lt)
            v.drain()
            v.tensor_mul(kml[:, :], kml[:, :], maskl[:, :])
            v.drain()
            v.tensor_mul(grankl[:, :], grankl[:, :], maskl[:, :])
            v.drain()
            v.sem_inc(v5, 1)
            ct["v5"] += 1
        V5_OWN = ct["v5"]

        @blk.tensor
        def _(t):
            t.wait_ge(v5, V5_OWN)
            t.matmul(pb[4][0:16, 0:NCOLL], cgsum16[:, :], kml[:, :],
                     start=True, stop=True)
            t.matmul(pb[5][0:16, 0:NCOLL], cgsum16[:, :], grankl[:, :],
                     start=True, stop=True)
            t.matmul(pb[6][0:16, 0:NCOLL], cgsumE[:, :], maskl[:, :],
                     start=True, stop=True).then_inc(t5, 1)
            ct["t5"] += 1
        T5_SEL = ct["t5"]

        @blk.vector
        def _(v):
            v.wait_ge(t5, T5_SEL)
            v.tensor_copy(t16a[:, :], pb[4][0:16, 0:NCOLL])
            v.tensor_copy(t16b[:, :], pb[5][0:16, 0:NCOLL])
            v.tensor_copy(g16[:, :], pb[6][0:16, 0:NCOLL])
            v.drain()
            v.tensor_single_scalar(s16[:, :], t16b[:, :], 640.0, ALU.is_ge)
            for m in range(2, 8):
                v.drain()
                v.tensor_single_scalar(t16a2[:, :], t16b[:, :], 640.0 * m, ALU.is_ge)
                v.drain()
                v.tensor_add(s16[:, :], s16[:, :], t16a2[:, :])
            v.drain()
            v.tensor_single_scalar(t16a2[:, :], t16b[:, :], 4480.0, ALU.is_ge)
            v.drain()
            v.tensor_single_scalar(il16[0:16, 0:NCOLL], t16b[:, :], 4864.0, ALU.is_ge)
            v.drain()
            v.scalar_tensor_tensor(t16b[:, :], s16[:, :], 4480.0, t16b[:, :],
                                   ALU.mult, ALU.add)
            v.drain()
            v.scalar_tensor_tensor(t16b[:, :], il16[0:16, 0:NCOLL], 2688.0,
                                   t16b[:, :], ALU.mult, ALU.add)
            v.drain()
            v.tensor_single_scalar(s16[:, :], t16a2[:, :], -256.0, ALU.mult)
            v.drain()
            v.tensor_single_scalar(s16[:, :], s16[:, :], 640.0, ALU.add)
            v.drain()
            v.scalar_tensor_tensor(s16[:, :], il16[0:16, 0:NCOLL], -128.0,
                                   s16[:, :], ALU.mult, ALU.add)
            v.drain()
            v.tensor_scalar_mul(g16[:, :], g16[:, :], 1.0 / 5120.0)
            v.drain()
            v.tensor_mul(g16[:, :], g16[:, :], s16[:, :])
            v.drain()
            v.tensor_add(g16[:, :], g16[:, :], t16b[:, :])
            v.drain()
            v.tensor_mul(g16[:, :], g16[:, :], t16a[:, :])
            v.tensor_mul(s16[:, :], tp16[:, :], t16a[:, :])
            v.drain()
            v.sem_inc(v5, 1)
            ct["v5"] += 1
        V5_G16 = ct["v5"]

        @blk.tensor
        def _(t):
            t.wait_ge(v5, V5_G16)
            for j, src in enumerate([g16, s16]):
                for sh in range(2):
                    ins = t.transpose(pb[4 + 2 * j + sh][0:128, 0:16],
                                src[:, sh * 128:(sh + 1) * 128],
                                cident[0:16, 0:16])
            ins.then_inc(t5, 1)
            ct["t5"] += 1
        T5_GT = ct["t5"]

        @blk.vector
        def _(v):
            v.wait_ge(t5, T5_GT)
            for sh in range(2):
                v.tensor_copy(gcols[:, sh * 16:(sh + 1) * 16], pb[4 + sh][0:128, 0:16])
                v.tensor_copy(scols[:, sh * 16:(sh + 1) * 16], pb[6 + sh][0:128, 0:16])
            v.drain()
            v.sem_inc(v5, 1)
            ct["v5"] += 1

        if stage == "routemath":
            @blk.scalar
            def _(s):
                s.wait_ge(v5, ct["v5"])
                dma(s, sto, out[0:128, :], fullsel[:, 0:1024])
                dma(s, sto, out[128:256, :], fullsel[:, 1024:2048])
                dma(s, sto, out[256:384, 0:NCOLL], grankl[:, :])
                dma(s, sto, out[384:400, 0:NCOLL], g16[:, :])
                dma(s, sto, out[400:416, 0:NCOLL], s16[:, :])
                dma(s, sto, out[416:544, 0:32], gcols.bitcast(F32)[:, :])
                dma(s, sto, out[544:672, 0:32], scols[:, :])
                dma(s, sto, out[672:688, 0:1024], i16f.bitcast(F32)[:, :])
                s.wait_ge(sto.h, 16 * sto.n)
            return nc

        @blk.sync
        def _(s):
            s.wait_ge(v5, V5_I16F)
            for e in range(8):
                dma(s, ldz, i16rep[16 * e:16 * (e + 1), :], i16f[:, :])
        LDZ_T = 16 * ldz.n

        @blk.gpsimd
        def _(g):
            g.wait_ge(ldz.h, LDZ_T)
            g.wait_ge(qpz.h, 16 * qpz.n)
            g.wait_ge(ldr.h, LDR_IREP)
            for k in range(8):
                g.dma_scatter_add(
                    idtab[k * (CAP + 1024):(k + 1) * (CAP + 1024), 0:1],
                    cgid[:, 32 * k:32 * (k + 1)].rearrange("p (n one) -> p n one", one=1),
                    i16rep[:, 256 * k:256 * (k + 1)],
                    num_idxs=TG // 8, num_idxs_reg=TG // 8, elem_size=1, elem_step=64,
                ).then_inc(qsc[k].h, 16)
                qsc[k].n += 1

        V5_IDW0 = ct["v5"]

        @blk.scalar
        def _(s):
            for k in range(8):
                s.wait_ge(qsc[k].h, 16 * qsc[k].n)
                if k >= 2:
                    s.wait_ge(v5, V5_IDW0 + k - 1)
                dma(s, qid[k % 2],
                    idwb[k % 2][:, :].rearrange("p (a c) -> p a c", c=64),
                    idtab[k * (CAP + 1024):(k + 1) * (CAP + 1024), :]
                    .rearrange("(a p) c -> p a c", p=128)[:, 0:NST, :])

        @blk.vector
        def _(v):
            for k in range(8):
                v.wait_ge(qid[k % 2].h, 16 * (k // 2 + 1))
                if k == 0:
                    v.tensor_copy(idw[:, 0:NST * 64:64],
                                  idwb[0][:, 0:NST * 64:64]).then_inc(v5, 1)
                else:
                    v.tensor_add(idw[:, 0:NST * 64:64], idw[:, 0:NST * 64:64],
                                 idwb[k % 2][:, 0:NST * 64:64]).then_inc(v5, 1)
                v.drain()
                ct["v5"] += 1
        V5_IDW8 = ct["v5"]

        @blk.vector
        def _(v):
            v.wait_ge(v5, V5_IDW8)
            v.tensor_single_scalar(zt[:, 0:NST], idw[:, 0:NST * 64:64], 1.0, ALU.subtract)
            v.drain()
            v.tensor_single_scalar(zt[:, 0:NST], zt[:, 0:NST], 0.0, ALU.max)
            v.drain()
            v.tensor_copy(idcols[:, :], zt[:, 0:NST])
            v.drain()
            v.sem_inc(v5, 1)
            ct["v5"] += 1
        V5_IDC = ct["v5"]

        barrier()

        regB.close()

        regC = contextlib.ExitStack()
        es.enter_context(regC)
        rct = lambda name, shape, dtt=F32: regC.enter_context(
            nc.sbuf_tensor(name, shape, dtt))
        w1sb = rct("w1sb", [128, DC * FBS])
        w2sb = rct("w2sb", [128, FCB * D])
        y1sb = rct("y1sb", [128, FCB * TSL])
        sfb = [rct(f"sfb{k}", [128, DC * TSL]) for k in range(2)]
        y2ev = [rct(f"y2ev{k}", [128, D]) for k in range(2)]
        y2pl = [rct(f"y2pl{k}", [128, D]) for k in range(2)]
        regP6 = contextlib.ExitStack()
        es.enter_context(regP6)
        slt = [regP6.enter_context(nc.sbuf_tensor(f"slt{k}", [128, D], F32))
               for k in range(2)]
        sle = [regP6.enter_context(nc.sbuf_tensor(f"sle{k}", [128, D], F32))
               for k in range(2)]

        TN_P6 = ct["tn"]

        @blk.gpsimd
        def _(g):
            g.wait_ge(cc, CC_H2)
            g.wait_ge(v5, V5_IDC)
            for n in range(NST):
                if n >= 2:
                    g.wait_ge(tn, TN_P6 + 8 * (n - 1))
                g.indirect_dma_start(
                    out=slt[n % 2][:, :],
                    out_offset=None,
                    in_=h2glob[:, :],
                    in_offset=IndirectOffsetOnAxis(ap=idcols[:, n:n + 1], axis=0),
                ).then_inc(qsl[n % 2].h, 16)
                qsl[n % 2].n += 1

        @blk.sync
        def _(s):
            for dcx in range(DC):
                dma(s, ldwb, w1sb[:, dcx * FBS:(dcx + 1) * FBS],
                    w1[dcx * 128:(dcx + 1) * 128, 0:FBS])
            for fc in range(FCB):
                dma(s, ldwb, w2sb[:, fc * D:(fc + 1) * D],
                    w2[fc * 128:(fc + 1) * 128, :])
        LDWB_0 = 16 * ldwb.n

        @blk.tensor
        def _(t):
            for n in range(NST):
                t.wait_ge(qsl[n % 2].h, 16 * (n // 2 + 1))
                for d in range(DC):
                    k = 8 * n + d
                    if k >= 2:
                        t.wait_ge(evB0 if k % 2 == 0 else evB1, k // 2)
                    t.transpose(pb[k % 2][:, 0:128],
                                slt[n % 2][:, d * 128:(d + 1) * 128],
                                cident[:, :]).then_inc(tn, 1)
        ct["tn"] += 8 * NST

        @blk.vector
        def _(v):
            for n in range(NST):
                for d in range(0, DC, 2):
                    k = 8 * n + d
                    v.wait_ge(tn, TN_P6 + k + 1)
                    if n >= 2 and d == 0:
                        v.wait_ge(stf[n % 2].h, 16 * 8 * (n // 2))
                    v.tensor_copy(sle[n % 2][:, d * 128:(d + 1) * 128],
                                  pb[k % 2][:, 0:128]).then_inc(evB0, 1)

        @blk.scalar
        def _(s):
            for n in range(NST):
                for d in range(1, DC, 2):
                    k = 8 * n + d
                    s.wait_ge(tn, TN_P6 + k + 1)
                    if n >= 2 and d == 1:
                        s.wait_ge(stf[n % 2].h, 16 * 8 * (n // 2))
                    s.activation(sle[n % 2][:, d * 128:(d + 1) * 128],
                                 pb[k % 2][:, 0:128], AF.Copy,
                                 bias=0.0, scale=1.0).then_inc(evB1, 1)
                s.wait_ge(evB0, 4 * (n + 1))
                s.wait_ge(evB1, 4 * (n + 1))
                for d in range(DC):
                    dma(s, stf[n % 2],
                        slabf[d * 128:(d + 1) * 128, n * 128:(n + 1) * 128],
                        sle[n % 2][:, d * 128:(d + 1) * 128])

        for fb in range(FBLK):
            if fb == 1:
                @blk.sync
                def _(s):
                    for dcx in range(DC):
                        dma(s, ldwb, w1sb[:, dcx * FBS:(dcx + 1) * FBS],
                            w1[dcx * 128:(dcx + 1) * 128, FBS:FF])
                    for fc in range(FCB):
                        dma(s, ldwb, w2sb[:, fc * D:(fc + 1) * D],
                            w2[FBS + fc * 128:FBS + (fc + 1) * 128, :])

            LDWB_FB = 16 * ldwb.n
            TN_FB = ct["tn"]

            @blk.sync
            def _(s):
                for ts_ in range(NSL):
                    if ts_ >= 2:
                        s.wait_ge(tn, TN_FB + 20 * (ts_ - 1))
                    if fb == 0:
                        ntile = 2 * ts_ + 2
                        s.wait_ge(stf[0].h, 16 * 8 * ((ntile + 1) // 2))
                        s.wait_ge(stf[1].h, 16 * 8 * (ntile // 2))
                    for d in range(DC):
                        dma(s, lsf[ts_ % 2], sfb[ts_ % 2][:, d * TSL:(d + 1) * TSL],
                            slabf[d * 128:(d + 1) * 128, ts_ * TSL:(ts_ + 1) * TSL])
                    if fb == 1:
                        for tc in range(2):
                            if ts_ >= 1:
                                s.wait_ge(evD, (NSL + ts_ - 1) * 4 + 2 * tc + 2)
                            dma(s, lyp[tc % 2], y2pl[tc % 2][:, :],
                                y2partial[ts_ * TSL + tc * 128:
                                          ts_ * TSL + (tc + 1) * 128, :])

            @blk.tensor
            def _(t):
                t.wait_ge(ldwb.h, LDWB_FB)
                for ts_ in range(NSL):
                    gs = fb * NSL + ts_
                    t.wait_ge(lsf[ts_ % 2].h, 16 * DC * (gs // 2 + 1))
                    for fc in range(FCB):
                        bank = 2 + (fc % 2)
                        gfc = (fb * NSL + ts_) * FCB + fc
                        if gfc >= 2:
                            t.wait_ge(evC, gfc - 1)
                        for dcx in range(DC):
                            ins = t.matmul(pb[bank][:, 0:TSL],
                                     w1sb[:, dcx * FBS + fc * 128:dcx * FBS + (fc + 1) * 128],
                                     sfb[ts_ % 2][:, dcx * TSL:(dcx + 1) * TSL],
                                     start=(dcx == 0), stop=(dcx == DC - 1))
                        ins.then_inc(tn, 1)
                    t.wait_ge(evC, (fb * NSL + ts_) * FCB + FCB)
                    for tc in range(2):
                        for dh in range(2):
                            bank = 4 + (2 * tc + dh) % 2
                            gi = (fb * NSL + ts_) * 4 + 2 * tc + dh
                            if gi >= 2:
                                t.wait_ge(evD, gi - 1)
                            for fc in range(FCB):
                                ins = t.matmul(pb[bank][:, :],
                                         y1sb[:, fc * TSL + tc * 128:fc * TSL + (tc + 1) * 128],
                                         w2sb[:, fc * D + dh * 512:fc * D + dh * 512 + 512],
                                         start=(fc == 0),
                                         stop=(fb == 1 and fc == FCB - 1))
                            if fb == 0:
                                ins = t.matmul(pb[bank][:, :], conesr[:, :],
                                         sb2r[:, dh * 512:dh * 512 + 512],
                                         start=False, stop=True)
                            ins.then_inc(tn, 1)
            ct["tn"] += 20 * NSL

            @blk.scalar
            def _(s):
                tgt = y2partial if fb == 0 else y2slab
                for ts_ in range(NSL):
                    for fc in range(FCB):
                        s.wait_ge(tn, TN_FB + 20 * ts_ + fc + 1)
                        s.activation(y1sb[:, fc * TSL:(fc + 1) * TSL],
                                     pb[2 + (fc % 2)][:, 0:TSL], AF.Relu,
                                     bias=sb1t[:, fb * FCB + fc:fb * FCB + fc + 1],
                                     scale=1.0).then_inc(evC, 1)
                    for tc in range(2):
                        s.wait_ge(evD, (fb * NSL + ts_) * 4 + 2 * tc + 2)
                        dma(s, sty[tc % 2],
                            tgt[ts_ * TSL + tc * 128:ts_ * TSL + (tc + 1) * 128, :],
                            y2ev[tc % 2][:, :])

            @blk.vector
            def _(v):
                for ts_ in range(NSL):
                    gs = fb * NSL + ts_
                    for tc in range(2):
                        if gs >= 1:
                            v.wait_ge(sty[tc % 2].h, 16 * gs)
                        for dh in range(2):
                            v.wait_ge(tn, TN_FB + 20 * ts_ + FCB + 2 * tc + dh + 1)
                            bank = 4 + (2 * tc + dh) % 2
                            if fb == 0:
                                v.tensor_copy(y2ev[tc % 2][:, dh * 512:dh * 512 + 512],
                                              pb[bank][:, :]).then_inc(evD, 1)
                            else:
                                v.wait_ge(lyp[tc % 2].h, 16 * (ts_ + 1))
                                v.tensor_add(y2ev[tc % 2][:, dh * 512:dh * 512 + 512],
                                             y2pl[tc % 2][:, dh * 512:dh * 512 + 512],
                                             pb[bank][:, :]).then_inc(evD, 1)

            if fb == 1:
                YB = [0, 640, 1280, 1920, 2560, 3200, 3840, 4480, 4864, 5120]

                @blk.gpsimd
                def _(g):
                    for k in range(9):
                        nslice = (YB[k + 1] + TSL - 1) // TSL
                        g.wait_ge(sty[0].h, 16 * (NSL + nslice))
                        g.wait_ge(sty[1].h, 16 * (NSL + nslice))
                        g.collective_compute(
                            "AllGather", ALU.bypass,
                            replica_groups=[list(range(NCORES))],
                            ins=[y2slab[YB[k]:YB[k + 1], :].opt()],
                            outs=[y2glob[8 * YB[k]:8 * YB[k + 1], :].opt()],
                        ).then_inc(cc, 1)
                ct["cc"] += 9
                CC_Y2 = ct["cc"]

            barrier()

        regP6.close()
        regC.close()


        regD = contextlib.ExitStack()
        es.enter_context(regD)
        gout = [regD.enter_context(nc.sbuf_tensor(f"gout{k}", [128, D], F32))
                for k in range(4)]
        x2b = [regD.enter_context(nc.sbuf_tensor(f"x2b{k}", [128, D], F32))
               for k in range(4)]
        otb = [regD.enter_context(nc.sbuf_tensor(f"otb{k}", [128, D], F32))
               for k in range(4)]

        x2v = x2d.rearrange("(r s) d -> r s d", s=16)
        outv = out.rearrange("(r s) d -> r s d", s=16)

        @blk.sync
        def _(s):
            for gi in range(32):
                if gi >= 4:
                    s.wait_ge(vO, gi - 3)
                q, sh = gi % 16, gi // 16
                dma(s, lx4[gi % 4], x2b[gi % 4][:, :],
                    x2v[sh * 128:(sh + 1) * 128, q, :])

        @blk.gpsimd
        def _(g):
            g.wait_ge(cc, CC_Y2)
            for gi in range(32):
                if gi >= 4:
                    g.wait_ge(vO, gi - 3)
                q, sh = gi % 16, gi // 16
                g.indirect_dma_start(
                    out=gout[gi % 4][:, :],
                    out_offset=None,
                    in_=y2glob[:, :],
                    in_offset=IndirectOffsetOnAxis(
                        ap=gcols[:, sh * 16 + q:sh * 16 + q + 1], axis=0),
                ).then_inc(qg4[gi % 4].h, 16)
                qg4[gi % 4].n += 1

        @blk.vector
        def _(v):
            for gi in range(32):
                q, sh = gi % 16, gi // 16
                v.wait_ge(lx4[gi % 4].h, 16 * (gi // 4 + 1))
                v.wait_ge(qg4[gi % 4].h, 16 * (gi // 4 + 1))
                if gi >= 4:
                    v.wait_ge(sp4[gi % 4].h, 16 * (gi // 4))
                v.scalar_tensor_tensor(otb[gi % 4][:, :], gout[gi % 4][:, :],
                                       scols[:, sh * 16 + q:sh * 16 + q + 1],
                                       x2b[gi % 4][:, :],
                                       ALU.mult, ALU.add).then_inc(vO, 1)

        @blk.scalar
        def _(s):
            for gi in range(32):
                q, sh = gi % 16, gi // 16
                s.wait_ge(vO, gi + 1)
                dma(s, sp4[gi % 4], outv[sh * 128:(sh + 1) * 128, q, :],
                    otb[gi % 4][:, :])
            for k in range(4):
                s.wait_ge(sp4[k].h, 16 * sp4[k].n)

    return nc


_NC_CACHE = {}


def _get_nc():
    if _STAGE not in _NC_CACHE:
        nc = _build(_STAGE)
        nc.finalize()
        _NC_CACHE[_STAGE] = nc
    return _NC_CACHE[_STAGE]


def kernel(L0_attn_hidden_states, L0_attn_W_q, L0_ffn_gate_w, L0_ffn_gate_b,
           L0_ffn_w1, L0_ffn_b1, L0_ffn_w2, L0_ffn_b2):
    x = np.asarray(L0_attn_hidden_states, dtype=np.float32)
    wqv = np.ascontiguousarray(np.asarray(L0_attn_W_q, dtype=np.float32))
    gw = np.asarray(L0_ffn_gate_w, dtype=np.float32)
    gbv = np.asarray(L0_ffn_gate_b, dtype=np.float32)
    w1v = np.asarray(L0_ffn_w1, dtype=np.float32)
    b1v = np.asarray(L0_ffn_b1, dtype=np.float32)
    w2v = np.asarray(L0_ffn_w2, dtype=np.float32)
    b2v = np.asarray(L0_ffn_b2, dtype=np.float32)

    consts = _host_consts()
    gwtv = np.ascontiguousarray(gw.T)
    gbr = np.ascontiguousarray(gbv.reshape(1, E))
    in_maps = []
    for c in range(NCORES):
        gsel = np.zeros((128, 16), np.float32)
        for q in range(16):
            gsel[16 * c + q, q] = 1.0
        clt = np.zeros((8, 1), np.float32)
        clt[:c, 0] = 1.0
        m = {
            "x": np.ascontiguousarray(x[c]),
            "wq": wqv,
            "gwt": gwtv,
            "gb": gbr,
            "w1": np.ascontiguousarray(w1v[c]),
            "b1t": np.ascontiguousarray(b1v[c].reshape(FF // 128, 128).T),
            "w2": np.ascontiguousarray(w2v[c]),
            "b2r": np.ascontiguousarray(b2v[c].reshape(1, D)),
            "gsel": gsel,
            "corelt": clt,
        }
        m.update(consts)
        in_maps.append(m)

    nc = _get_nc()
    res = run_bass_kernel_spmd(nc, in_maps, core_ids=list(range(NCORES)))
    outp = np.stack([np.asarray(r["out"]) for r in res.results], axis=0)
    return outp.astype(np.float32)


# version 17
# speedup vs baseline: 1.0585x; 1.0585x over previous
import contextlib
import numpy as np

import concourse.bacc as bacc
import concourse.mybir as mybir
from concourse.bass_utils import run_bass_kernel_spmd
from concourse.bass_core_rust import IndirectOffsetOnAxis

F32 = mybir.dt.float32
I16 = mybir.dt.int16
U32 = mybir.dt.uint32
AF = mybir.ActivationFunctionType
ALU = mybir.AluOpType
AX = mybir.AxisListType

B, S, D = 8, 4096, 1024
NCORES = 8
TL = S
NT = TL // 128
DC = D // 128
E = 8
FF = 4096
CAP = 5120
TG = B * S
NCOL = TG // 16
NCOLL = TL // 16
EPS = 1e-5
TRASH = CAP
FBLK = 2
FBS = FF // FBLK
FCB = FBS // 128
TSL = 256
NSL = CAP // TSL
NST = CAP // 128
NSC = 32

_STAGE = "full"


def _host_consts():
    ident = np.eye(128, dtype=np.float32)
    btriu = np.zeros((128, 128), np.float32)
    for e in range(8):
        for q2 in range(16):
            for q in range(16):
                if q2 < q:
                    btriu[16 * e + q2, 16 * e + q] = 1.0
    gspread = np.zeros((128, 128), np.float32)
    for m in range(128):
        gspread[16 * (m // 16) + 15, m] = 1.0
    gsum16 = np.zeros((128, 16), np.float32)
    gsumE = np.zeros((128, 16), np.float32)
    for e in range(8):
        for q in range(16):
            gsum16[16 * e + q, q] = 1.0
            gsumE[16 * e + q, q] = float(CAP * e)
    sel15 = np.zeros((128, 8), np.float32)
    for e in range(8):
        sel15[16 * e + 15, e] = 1.0
    grp8 = np.zeros((8, 128), np.float32)
    for e in range(8):
        grp8[e, 16 * e:16 * (e + 1)] = 1.0
    econst = (np.arange(128, dtype=np.float32) // 16).reshape(128, 1)
    gid = np.zeros((128, 256), np.float32)
    for i in range(TG // NCORES // 16 * 128 if False else 32768):
        gid[i % 128, i // 128] = float(i + 1)
    ctrash = np.zeros((128, 2048), np.float32)
    for p in range(128):
        q = p % 16
        for j in range(2048):
            ctrash[p, j] = float(CAP + ((16 * j + q) % 1024))
    onesr = np.ones((1, 128), np.float32)
    epsv = np.full((128, 1), EPS, np.float32)
    return dict(ident=ident, btriu=btriu, gspread=gspread, gsum16=gsum16,
                gsumE=gsumE, sel15=sel15, grp8=grp8, econst=econst,
                onesr=onesr, epsv=epsv, gid=gid, ctrash=ctrash,
                zeros=np.zeros((128, 1024), np.float32))


def _build(stage):
    nc = bacc.Bacc("TRN2", target_bir_lowering=False)
    dt = nc.dram_tensor

    x_in = dt("x", [TL, D], F32, kind="ExternalInput").ap()
    wq = dt("wq", [D, D], F32, kind="ExternalInput").ap()
    gwt = dt("gwt", [D, E], F32, kind="ExternalInput").ap()
    gb = dt("gb", [1, E], F32, kind="ExternalInput").ap()
    w1 = dt("w1", [D, FF], F32, kind="ExternalInput").ap()
    b1t = dt("b1t", [128, FF // 128], F32, kind="ExternalInput").ap()
    w2 = dt("w2", [FF, D], F32, kind="ExternalInput").ap()
    b2r = dt("b2r", [1, D], F32, kind="ExternalInput").ap()
    gsel = dt("gsel", [128, 16], F32, kind="ExternalInput").ap()
    corelt = dt("corelt", [8, 1], F32, kind="ExternalInput").ap()
    C = {k: dt(k, list(v.shape), F32, kind="ExternalInput").ap()
         for k, v in _host_consts().items()}
    out = dt("out", [TL, D], F32, kind="ExternalOutput").ap()

    x2d = dt("x2d", [TL, D], F32, kind="Internal").ap()
    h2shard = dt("h2shard", [TL + 5, D], F32, kind="Internal").ap()
    h2glob = dt("h2glob", [TG, D], F32, kind="Internal").ap()
    idxg2 = dt("idxg2", [NCORES * 5, D], F32, kind="Internal").ap()
    cnt8 = dt("cnt8", [8, 1], F32, kind="Internal").ap()
    cnttab = dt("cnttab", [8, 8], F32, kind="Internal").ap()
    idtab = dt("idtab", [8 * (CAP + 1024), 64], F32, kind="Internal").ap()
    slabf = dt("slabf", [D, CAP], F32, kind="Internal").ap()
    y2partial = dt("y2partial", [CAP, D], F32, kind="Internal").ap()
    y2slab = dt("y2slab", [CAP, D], F32, kind="Internal").ap()
    y2glob = dt("y2glob", [NCORES * CAP, D], F32, kind="Internal").ap()

    es = contextlib.ExitStack()
    with es:
        sbt = lambda name, shape, dtt=F32: es.enter_context(
            nc.sbuf_tensor(name, shape, dtt))
        sem = lambda name: es.enter_context(nc.semaphore(name))
        pb = [es.enter_context(nc.psum_tensor(f"pb{k}", [128, 512], F32))
              for k in range(8)]

        cident = sbt("cident", [128, 128])
        cbtriu = sbt("cbtriu", [128, 128])
        cgspread = sbt("cgspread", [128, 128])
        cgsum16 = sbt("cgsum16", [128, 16])
        cgsumE = sbt("cgsumE", [128, 16])
        csel15 = sbt("csel15", [128, 8])
        cgrp8 = sbt("cgrp8", [8, 128])
        cecon = sbt("cecon", [128, 1])
        conesr = sbt("conesr", [1, 128])
        cepsv = sbt("cepsv", [128, 1])
        cgsel = sbt("cgsel", [128, 16])
        idcols = sbt("idcols", [128, NST], U32)
        ccorelt = sbt("ccorelt", [8, 1])
        sgwt = sbt("sgwt", [128, DC * E])
        sgb = sbt("sgb", [1, E])
        sb2r = sbt("sb2r", [1, D])
        sb1t = sbt("sb1t", [128, FF // 128])
        mcol = sbt("mcol", [128, NT])
        rcol = sbt("rcol", [128, NT])
        ncol = sbt("ncol", [128, NT])
        tcol = sbt("tcol", [128, NT])
        idxl = sbt("idxl", [128, NT])
        tpl = sbt("tpl", [128, NT])
        il16 = sbt("il16", [16, 320])
        tp16 = sbt("tp16", [16, NCOLL])
        g16 = sbt("g16", [16, NCOLL])
        s16 = sbt("s16", [16, NCOLL])
        gcols = sbt("gcols", [128, 32], U32)
        maskl = sbt("maskl", [128, NCOLL])
        colpl = sbt("colpl", [128, NCOLL])
        rexl = sbt("rexl", [128, NCOLL])
        cnts = sbt("cnts", [8, 1])
        scols = sbt("scols", [128, 32])

        cc = sem("cc")
        tn = sem("tn"); bar = sem("bar")
        vst = sem("vst"); ssq = sem("ssq"); vrs = sem("vrs"); sh1 = sem("sh1")
        evA = sem("evA"); evH = sem("evH"); evF = sem("evF")
        vX2 = sem("vX2"); vSm = sem("vSm"); sEx = sem("sEx")
        v5 = sem("v5"); t5 = sem("t5")
        evB0 = sem("evB0"); evB1 = sem("evB1")
        evC = sem("evC"); evD = sem("evD"); vO = sem("vO")

        class Q:
            def __init__(self, name, eng):
                self.h = sem(name)
                self.n = 0
                self.eng = eng
        QREG = []
        def mkq(name, eng):
            q = Q(name, eng); QREG.append(q); return q
        def dma(engapi, q, out_ap, in_ap):
            engapi.dma_start(out=out_ap, in_=in_ap).then_inc(q.h, 16)
            q.n += 1
            return q.n
        ldc = mkq("ldc", "sync")
        ldx = [mkq(f"ldx{k}", "sync") for k in range(2)]
        ldw = [mkq(f"ldw{k}", "sync") for k in range(2)]
        ldr = mkq("ldr", "sync")
        ldz = mkq("ldz", "sync")
        lds = [mkq(f"lds{k}", "sync") for k in range(2)]
        lsl = [mkq(f"lsl{k}", "sync") for k in range(2)]
        ldwb = mkq("ldwb", "sync")
        lsf = [mkq(f"lsf{k}", "sync") for k in range(2)]
        lyp = [mkq(f"lyp{k}", "sync") for k in range(2)]
        lx2 = [mkq(f"lx2{k}", "sync") for k in range(2)]
        stx = mkq("stx", "scalar")
        smh = mkq("smh", "scalar")
        stf = [mkq(f"stf{k}", "scalar") for k in range(2)]
        sty = [mkq(f"sty{k}", "scalar") for k in range(2)]
        sid = mkq("sid", "scalar")
        scn = mkq("scn", "scalar")
        sto = mkq("sto", "scalar")
        sop = [mkq(f"sop{k}", "scalar") for k in range(2)]
        qsc = [mkq(f"qsc{k}", "gpsimd") for k in range(8)]
        qid = [mkq(f"qid{k}", "sync") for k in range(2)]
        qpz = mkq("qpz", "scalar")
        qsl = [mkq(f"qsl{k}", "gpsimd") for k in range(2)]
        qg4 = [mkq(f"qg4_{k}", "gpsimd") for k in range(4)]
        lx4 = [mkq(f"lx4_{k}", "sync") for k in range(4)]
        sp4 = [mkq(f"sp4_{k}", "scalar") for k in range(4)]
        qgi = [mkq(f"qgi{k}", "gpsimd") for k in range(2)]

        ct = {"cc": 0, "tn": 0, "t5": 0, "v5": 0}
        NBAR = [0]
        blk = es.enter_context(nc.Block())

        def barrier():
            NBAR[0] += 1
            tgt = 5 * NBAR[0]
            snap = [(q, q.n) for q in QREG if q.n > 0]

            @blk.sync
            def _(s):
                for q, n in snap:
                    if q.eng == "sync":
                        s.wait_ge(q.h, 16 * n)
                s.sem_inc(bar, 1)
                s.wait_ge(bar, tgt)

            @blk.tensor
            def _(t):
                t.drain()
                t.sem_inc(bar, 1)
                t.wait_ge(bar, tgt)

            @blk.vector
            def _(v):
                v.drain()
                v.sem_inc(bar, 1)
                v.wait_ge(bar, tgt)

            @blk.scalar
            def _(s):
                s.drain()
                for q, n in snap:
                    if q.eng == "scalar":
                        s.wait_ge(q.h, 16 * n)
                s.sem_inc(bar, 1)
                s.wait_ge(bar, tgt)

            @blk.gpsimd
            def _(g):
                for q, n in snap:
                    if q.eng == "gpsimd":
                        g.wait_ge(q.h, 16 * n)
                g.sem_inc(bar, 1)
                g.wait_ge(bar, tgt)

        @blk.sync
        def _(s):
            for apto, apfrom in [
                (cident, C["ident"]), (cbtriu, C["btriu"]),
                (cgspread, C["gspread"]), (cgsum16, C["gsum16"]),
                (cgsumE, C["gsumE"]), (csel15, C["sel15"]),
                (cgrp8, C["grp8"]), (cecon, C["econst"]),
                (conesr, C["onesr"]), (cepsv, C["epsv"]),
                (cgsel, gsel), (ccorelt, corelt), (sgb, gb),
                (sb2r, b2r), (sb1t, b1t),
            ]:
                dma(s, ldc, apto[:, :], apfrom)
            for k in range(DC):
                dma(s, ldc, sgwt[:, k * E:(k + 1) * E],
                    gwt[k * 128:(k + 1) * 128, :])
        flatz = idtab.rearrange("r c -> (r c)")
        @blk.scalar
        def _(s):
            for k in range(8):
                base = k * (CAP + 1024) * 64
                for n in range(3):
                    dma(s, qpz,
                        flatz[base + n * 131072:base + (n + 1) * 131072]
                        .rearrange("(p c) -> p c", p=128),
                        C["zeros"])
        LDC_T = 16 * ldc.n

        regA = contextlib.ExitStack()
        es.enter_context(regA)
        ra = lambda name, shape, dtt=F32: regA.enter_context(
            nc.sbuf_tensor(name, shape, dtt))
        h1 = ra("h1", [128, NT * D])
        MG = ra("MG", [128, DC * D])
        xt = [ra(f"xt{k}", [128, D]) for k in range(2)]
        x2t = ra("x2t", [128, D])
        h2t = ra("h2t", [128, D])
        sqt = ra("sqt", [128, D])
        h1T = [ra(f"h1T{k}", [128, D]) for k in range(2)]
        h2T = ra("h2T", [128, D])
        wqb = [sqt[:, k * 128:(k + 1) * 128] for k in range(2)]
        lgt = ra("lgt", [128, 16])
        aidx = ra("aidx", [128, 8], U32)

        @blk.sync
        def _(s):
            for i in range(NT):
                if i >= 2:
                    s.wait_ge(sh1, i - 1)
                dma(s, ldx[i % 2], xt[i % 2][:, :],
                    x_in[i * 128:(i + 1) * 128, :])

        @blk.vector
        def _(v):
            for i in range(NT):
                v.wait_ge(ldx[i % 2].h, 16 * (i // 2 + 1))
                v.reduce_sum(tcol[:, i:i + 1], xt[i % 2][:, :], axis=AX.X)
                v.tensor_mul(sqt[:, :], xt[i % 2][:, :], xt[i % 2][:, :])
                v.drain()
                v.reduce_sum(mcol[:, i:i + 1], sqt[:, :], axis=AX.X)
                v.drain()
                v.tensor_scalar_mul(ncol[:, i:i + 1], tcol[:, i:i + 1], 1.0 / D)
                v.drain()
                v.tensor_scalar_mul(tcol[:, i:i + 1], mcol[:, i:i + 1], 1.0 / D)
                v.drain()
                v.tensor_mul(mcol[:, i:i + 1], ncol[:, i:i + 1], ncol[:, i:i + 1])
                v.drain()
                v.tensor_sub(tcol[:, i:i + 1], tcol[:, i:i + 1], mcol[:, i:i + 1])
                v.drain()
                v.tensor_copy(mcol[:, i:i + 1], ncol[:, i:i + 1]).then_inc(vst, 1)
                v.wait_ge(ssq, i + 1)
                v.reciprocal(rcol[:, i:i + 1], tcol[:, i:i + 1])
                v.drain()
                v.tensor_mul(ncol[:, i:i + 1], mcol[:, i:i + 1], rcol[:, i:i + 1])
                v.drain()
                v.tensor_scalar_mul(ncol[:, i:i + 1], ncol[:, i:i + 1], -1.0).then_inc(vrs, 1)

        @blk.scalar
        def _(s):
            s.wait_ge(ldc.h, LDC_T)
            for i in range(NT):
                s.wait_ge(vst, i + 1)
                s.activation(tcol[:, i:i + 1], tcol[:, i:i + 1], AF.Sqrt,
                             bias=cepsv[:, :], scale=1.0).then_inc(ssq, 1)
                s.wait_ge(vrs, i + 1)
                s.activation(h1[:, i * D:(i + 1) * D], xt[i % 2][:, :], AF.Identity,
                             bias=ncol[:, i:i + 1], scale=rcol[:, i:i + 1]).then_inc(sh1, 1)

        @blk.tensor
        def _(t):
            for half in range(2):
                if half == 1:
                    t.wait_ge(v5, ct["v5"] + 1)
                for i in range(NT):
                    if half == 0:
                        t.wait_ge(sh1, i + 1)
                    for d1 in range(DC):
                        ins = t.matmul(pb[d1][:, :],
                                 h1[:, i * D + d1 * 128:i * D + (d1 + 1) * 128],
                                 h1[:, i * D + half * 512:i * D + half * 512 + 512],
                                 start=(i == 0), stop=(i == NT - 1))
                ins.then_inc(tn, 1)
                ct["tn"] += 1
        TN_GRAM0, TN_GRAM1 = ct["tn"] - 1, ct["tn"]

        @blk.vector
        def _(v):
            for half in range(2):
                v.wait_ge(tn, TN_GRAM0 + half)
                for d1 in range(DC):
                    ins = v.tensor_copy(
                        MG[:, d1 * D + half * 512:d1 * D + half * 512 + 512],
                        pb[d1][:, :])
                ins.then_inc(v5, 1)
                ct["v5"] += 1
        V5_GRAM = ct["v5"]

        @blk.sync
        def _(s):
            s.wait_ge(sh1, NT)
            n = 0
            for half in range(2):
                for dout in range(DC):
                    for k in range(DC):
                        if n >= 2:
                            s.wait_ge(tn, TN_GRAM1 + n - 1)
                        dma(s, ldw[n % 2], wqb[n % 2][:, :],
                            wq[k * 128:(k + 1) * 128, dout * 128:(dout + 1) * 128])
                        n += 1

        @blk.tensor
        def _(t):
            n = 0
            for half in range(2):
                t.wait_ge(v5, V5_GRAM - 1 + half + 1)
                for dout in range(DC):
                    for k in range(DC):
                        t.wait_ge(ldw[n % 2].h, 16 * (n // 2 + 1))
                        t.matmul(pb[dout][:, :], wqb[n % 2][:, :],
                                 MG[:, k * D + half * 512:k * D + half * 512 + 512],
                                 start=(k == 0), stop=(k == DC - 1)).then_inc(tn, 1)
                        ct["tn"] += 1
                        n += 1
        TN_P3 = ct["tn"]

        @blk.vector
        def _(v):
            for half in range(2):
                v.wait_ge(tn, TN_GRAM1 + 64 * (half + 1))
                for dout in range(DC):
                    ins = v.tensor_copy(
                        MG[:, dout * D + half * 512:dout * D + half * 512 + 512],
                        pb[dout][:, :])
                ins.then_inc(v5, 1)
                ct["v5"] += 1
        V5_P3 = ct["v5"]

        barrier()

        @blk.sync
        def _(s):
            for i in range(NT):
                if i >= 2:
                    s.wait_ge(vX2, i - 1)
                dma(s, ldx[i % 2], xt[i % 2][:, :],
                    x_in[i * 128:(i + 1) * 128, :])

        TN_P4 = ct["tn"]
        NSTO = 3 if stage == "attn" else 2

        p_tr = {}; p_attn = {}; p_h2t = {}; p_gate = {}
        _c = TN_P4
        p_tr[0] = _c; _c += 8
        for i in range(NT):
            p_attn[i] = _c; _c += 2
            if i + 1 < NT:
                p_tr[i + 1] = _c; _c += 8
            if i >= 1:
                p_h2t[i - 1] = _c; _c += 8
                p_gate[i - 1] = _c; _c += 1
        p_h2t[NT - 1] = _c; _c += 8
        p_gate[NT - 1] = _c; _c += 1
        TN_P4_END = _c

        def emit_tr(t, i):
            for d in range(DC):
                k = 8 * i + d
                if k >= 2:
                    t.wait_ge(evA, k - 1)
                t.transpose(pb[6 + (k % 2)][:, 0:128],
                            h1[:, i * D + d * 128:i * D + (d + 1) * 128],
                            cident[:, :]).then_inc(tn, 1)

        def emit_attn(t, i):
            t.wait_ge(evA, 8 * (i + 1))
            if i >= 1:
                t.wait_ge(vX2, i)
            for half in range(2):
                for d1 in range(DC):
                    ins = t.matmul(pb[half][:, :],
                             h1T[i % 2][:, d1 * 128:(d1 + 1) * 128],
                             MG[:, d1 * D + half * 512:d1 * D + half * 512 + 512],
                             start=(d1 == 0), stop=(d1 == DC - 1))
                ins.then_inc(tn, 1)

        def emit_h2t(t, i):
            t.wait_ge(sh1, NT + i + 1)
            for d in range(DC):
                k = 8 * i + d
                if k >= 2:
                    t.wait_ge(evH, k - 1)
                t.transpose(pb[3 + (k % 2)][:, 0:128],
                            h2t[:, d * 128:(d + 1) * 128],
                            cident[:, :]).then_inc(tn, 1)

        def emit_gate(t, i):
            t.wait_ge(evH, 8 * (i + 1))
            if i >= 1:
                t.wait_ge(vSm, i)
            for d in range(DC):
                t.matmul(pb[2][:, 0:E], h2T[:, d * 128:(d + 1) * 128],
                         sgwt[:, d * E:(d + 1) * E],
                         start=(d == 0), stop=False)
            t.matmul(pb[2][:, 0:E], conesr[:, :], sgb[:, :],
                     start=False, stop=True).then_inc(tn, 1)

        @blk.tensor
        def _(t):
            t.wait_ge(ldc.h, LDC_T)
            emit_tr(t, 0)
            for i in range(NT):
                emit_attn(t, i)
                if i + 1 < NT:
                    emit_tr(t, i + 1)
                if i >= 1:
                    emit_h2t(t, i - 1)
                    emit_gate(t, i - 1)
            emit_h2t(t, NT - 1)
            emit_gate(t, NT - 1)
        ct["tn"] = TN_P4_END

        @blk.vector
        def _(v):
            def evac_tr(v, i):
                for d in range(DC):
                    v.wait_ge(tn, p_tr[i] + d + 1)
                    if i >= 2 and d == 0:
                        v.wait_ge(tn, p_attn[i - 2] + 2)
                    v.tensor_copy(h1T[i % 2][:, d * 128:(d + 1) * 128],
                                  pb[6 + ((8 * i + d) % 2)][:, 0:128]).then_inc(evA, 1)

            def gate_soft(v, i):
                v.wait_ge(tn, p_gate[i] + 1)
                v.tensor_copy(lgt[:, 0:E], pb[2][:, 0:E])
                v.drain()
                v.max(lgt[:, 8:16], lgt[:, 0:E])
                v.drain()
                v.max_index(aidx[:, 0:8], lgt[:, 8:16], lgt[:, 0:E])
                v.drain()
                v.tensor_copy(idxl[:, i:i + 1], aidx[:, 0:1])
                v.tensor_scalar_mul(tcol[:, i:i + 1], lgt[:, 8:9], -1.0)
                v.drain()
                v.sem_inc(vSm, 1)
                v.wait_ge(sEx, i + 1)
                v.reduce_sum(tpl[:, i:i + 1], lgt[:, 0:E], axis=AX.X)
                v.drain()
                v.reciprocal(tpl[:, i:i + 1], tpl[:, i:i + 1])
                v.drain()

            evac_tr(v, 0)
            for i in range(NT):
                v.wait_ge(ldx[i % 2].h, 16 * (NT // 2 + i // 2 + 1))
                if i >= 1:
                    v.wait_ge(stx.h, 16 * i)
                for half in range(2):
                    v.wait_ge(tn, p_attn[i] + half + 1)
                    v.tensor_add(x2t[:, half * 512:half * 512 + 512],
                                 xt[i % 2][:, half * 512:half * 512 + 512],
                                 pb[half][:, :])
                v.drain()
                v.sem_inc(vX2, 1)
                if i + 1 < NT:
                    evac_tr(v, i + 1)
                v.reduce_sum(tcol[:, i:i + 1], x2t[:, :], axis=AX.X)
                v.tensor_mul(sqt[:, :], x2t[:, :], x2t[:, :])
                v.drain()
                v.reduce_sum(mcol[:, i:i + 1], sqt[:, :], axis=AX.X)
                v.drain()
                v.tensor_scalar_mul(ncol[:, i:i + 1], tcol[:, i:i + 1], 1.0 / D)
                v.drain()
                v.tensor_scalar_mul(tcol[:, i:i + 1], mcol[:, i:i + 1], 1.0 / D)
                v.drain()
                v.tensor_mul(mcol[:, i:i + 1], ncol[:, i:i + 1], ncol[:, i:i + 1])
                v.drain()
                v.tensor_sub(tcol[:, i:i + 1], tcol[:, i:i + 1], mcol[:, i:i + 1])
                v.drain()
                v.tensor_copy(mcol[:, i:i + 1], ncol[:, i:i + 1]).then_inc(vst, 1)
                v.wait_ge(ssq, NT + i + 1)
                v.reciprocal(rcol[:, i:i + 1], tcol[:, i:i + 1])
                v.drain()
                v.tensor_mul(ncol[:, i:i + 1], mcol[:, i:i + 1], rcol[:, i:i + 1])
                v.drain()
                v.tensor_scalar_mul(ncol[:, i:i + 1], ncol[:, i:i + 1], -1.0).then_inc(vrs, 1)
                if i >= 1:
                    gate_soft(v, i - 1)
            gate_soft(v, NT - 1)

        @blk.scalar
        def _(s):
            def h2evac(s, i):
                for d in range(DC):
                    s.wait_ge(tn, p_h2t[i] + d + 1)
                    s.activation(h2T[:, d * 128:(d + 1) * 128],
                                 pb[3 + ((8 * i + d) % 2)][:, 0:128], AF.Copy,
                                 bias=0.0, scale=1.0).then_inc(evH, 1)

            def expx(s, i):
                s.wait_ge(vSm, i + 1)
                s.activation(lgt[:, 0:E], lgt[:, 0:E], AF.Exp,
                             bias=tcol[:, i:i + 1], scale=1.0)
                s.drain()
                s.sem_inc(sEx, 1)

            for i in range(NT):
                s.wait_ge(vst, NT + i + 1)
                s.activation(tcol[:, i:i + 1], tcol[:, i:i + 1], AF.Sqrt,
                             bias=cepsv[:, :], scale=1.0).then_inc(ssq, 1)
                if i >= 1:
                    h2evac(s, i - 1)
                    expx(s, i - 1)
                s.wait_ge(vrs, NT + i + 1)
                if i >= 1:
                    s.wait_ge(tn, p_h2t[i - 1] + 8)
                    s.wait_ge(smh.h, 16 * i)
                s.activation(h2t[:, :], x2t[:, :], AF.Identity,
                             bias=ncol[:, i:i + 1], scale=rcol[:, i:i + 1]).then_inc(sh1, 1)
                s.drain()
                dma(s, stx, x2d[i * 128:(i + 1) * 128, :], x2t[:, :])
                if stage == "attn":
                    dma(s, sto, out[i * 128:(i + 1) * 128, :], x2t[:, :])
                dma(s, smh, h2shard[i * 128:(i + 1) * 128, :], h2t[:, :])
            h2evac(s, NT - 1)
            expx(s, NT - 1)

        @blk.gpsimd
        def _(g):
            for k in range(3):
                g.wait_ge(smh.h, 16 * 8 * (k + 1))
                g.collective_compute(
                    "AllGather", ALU.bypass,
                    replica_groups=[list(range(NCORES))],
                    ins=[h2shard[1024 * k:1024 * (k + 1), :].opt()],
                    outs=[h2glob[8192 * k:8192 * (k + 1), :].opt()],
                ).then_inc(cc, 1)
        ct["cc"] += 3

        if stage == "attn":
            @blk.scalar
            def _(s):
                s.wait_ge(sto.h, 16 * sto.n)
            return nc

        barrier()

        TN_F = ct["tn"]

        @blk.tensor
        def _(t):
            for j in range(16):
                if j >= 2:
                    t.wait_ge(evF, j - 1)
                src = idxl if j < 8 else tpl
                ph = j % 8
                t.matmul(pb[5 + (j % 2)][0:16, 0:NT],
                         cident[:, 16 * ph:16 * (ph + 1)], src[:, :],
                         start=True, stop=True).then_inc(tn, 1)
        ct["tn"] += 16

        @blk.vector
        def _(v):
            for j in range(16):
                dst = il16 if j < 8 else tp16
                ph = j % 8
                v.wait_ge(tn, TN_F + j + 1)
                v.tensor_copy(dst[0:16, ph:NCOLL:8],
                              pb[5 + (j % 2)][0:16, 0:NT]).then_inc(evF, 1)

        irepl = sqt[:, 0:NCOLL]
        onesl = sqt[:, NCOLL:2 * NCOLL]
        colsuml = sqt[:, 2 * NCOLL:3 * NCOLL]
        cumsl = sqt[:, 3 * NCOLL:4 * NCOLL]

        @blk.sync
        def _(s):
            s.wait_ge(evF, 8)
            for e in range(8):
                dma(s, ldr, irepl[16 * e:16 * (e + 1), :], il16[:, 0:NCOLL])
        LDR_IREPL = 16 * ldr.n

        @blk.vector
        def _(v):
            v.memset(onesl[:, :], 1.0)
            v.memset(il16[:, NCOLL:320], 0.0)
            v.wait_ge(ldr.h, LDR_IREPL)
            v.wait_ge(ldc.h, LDC_T)
            v.tensor_single_scalar(maskl[:, :], irepl[:, :], cecon[:, :], ALU.is_equal)
            v.drain()
            v.sem_inc(v5, 1)
            ct["v5"] += 1
        V5_MASKL = ct["v5"]

        @blk.tensor
        def _(t):
            t.wait_ge(v5, V5_MASKL)
            t.matmul(pb[4][0:128, 0:NCOLL], cbtriu[:, :], maskl[:, :],
                     start=True, stop=True).then_inc(t5, 1)
            ct["t5"] += 1
        T5_COLPL = ct["t5"]

        @blk.vector
        def _(v):
            v.wait_ge(t5, T5_COLPL)
            v.tensor_copy(colpl[:, :], pb[4][0:128, 0:NCOLL])
            v.drain()
            v.tensor_add(colsuml[:, :], colpl[:, :], maskl[:, :])
            v.drain()
            v.tensor_tensor_scan(cumsl[:, :], onesl[:, :], colsuml[:, :], 0.0,
                                 ALU.mult, ALU.add)
            v.drain()
            v.tensor_sub(rexl[:, :], cumsl[:, :], colsuml[:, :])
            v.drain()
            v.sem_inc(v5, 1)
            ct["v5"] += 1
        V5_SCANL = ct["v5"]

        @blk.tensor
        def _(t):
            t.wait_ge(v5, V5_SCANL)
            t.matmul(pb[4][0:128, 0:NCOLL], cgspread[:, :], rexl[:, :],
                     start=True, stop=True)
            t.matmul(pb[5][0:8, 0:1], csel15[:, :], cumsl[:, NCOLL - 1:NCOLL],
                     start=True, stop=True).then_inc(t5, 1)
            ct["t5"] += 1
        T5_SPREADL = ct["t5"]

        @blk.vector
        def _(v):
            v.wait_ge(t5, T5_SPREADL)
            v.tensor_copy(rexl[:, :], pb[4][0:128, 0:NCOLL])
            v.tensor_copy(cnts[:, :], pb[5][0:8, 0:1])
            v.drain()
            v.tensor_copy(il16[0:8, NCOLL:NCOLL + 1], cnts[:, :])
            v.drain()
            v.sem_inc(v5, 1)
            ct["v5"] += 1
        V5_CNT = ct["v5"]

        @blk.scalar
        def _(s):
            s.wait_ge(v5, V5_CNT)
            dma(s, sid,
                h2shard[TL:TL + 5, :].rearrange("a b -> (a b)")
                .rearrange("(q s) -> q s", s=320),
                il16[:, :])

        @blk.gpsimd
        def _(g):
            g.wait_ge(sid.h, 16 * sid.n)
            g.collective_compute(
                "AllGather", ALU.bypass,
                replica_groups=[list(range(NCORES))],
                ins=[h2shard[TL:TL + 5, :].opt()],
                outs=[idxg2.opt()],
            ).then_inc(cc, 1)
            g.wait_ge(smh.h, 16 * 8 * 4)
            g.collective_compute(
                "AllGather", ALU.bypass,
                replica_groups=[list(range(NCORES))],
                ins=[h2shard[3072:TL, :].opt()],
                outs=[h2glob[3 * 8192:4 * 8192, :].opt()],
            ).then_inc(cc, 1)
        ct["cc"] += 2

        barrier()
        regA.close()

        if stage == "p4b":
            @blk.scalar
            def _(s):
                dma(s, sto, out[0:16, 0:NCOLL], il16[:, :])
                dma(s, sto, out[16:32, 0:NCOLL], tp16[:, :])
                s.wait_ge(sto.h, 16 * sto.n)
            return nc

        regB = contextlib.ExitStack()
        es.enter_context(regB)
        rbt = lambda name, shape, dtt=F32: regB.enter_context(
            nc.sbuf_tensor(name, shape, dtt))
        irep = rbt("irep", [128, NCOL])
        maskg = rbt("maskg", [128, NCOL])
        colp = rbt("colp", [128, NCOL])
        colsum = rbt("colsum", [128, NCOL])
        cums = rbt("cums", [128, NCOL])
        rex = rbt("rex", [128, NCOL])
        grank = rbt("grank", [128, NCOL])
        onesw = rbt("onesw", [128, NCOL])
        fullsel = rbt("fullsel", [128, NCOL])
        i16f = rbt("i16f", [16, NCOL], I16)
        i16rep = rbt("i16rep", [128, NCOL], I16)
        ctrashr = rbt("ctrashr", [128, NCOL])
        idw = rbt("idw", [128, NST * 64])
        idwb = [rbt(f"idwb{k}", [128, NST * 64]) for k in range(2)]
        cgid = rbt("cgid", [128, 256])
        grankl = rbt("grankl", [128, NCOLL])
        kml = rbt("kml", [128, NCOLL])
        pcb = rbt("pcb", [128, 1])
        ctab = rbt("ctab", [8, 8])
        t16a = rbt("t16a", [16, NCOLL])
        t16a2 = rbt("t16a2", [16, NCOLL])
        t16b = rbt("t16b", [16, NCOLL])
        zt = rbt("zt", [128, D])

        CC_IDX = 4
        CC_H2 = 5

        idxt3 = (idxg2.rearrange("a b -> (a b)")
                 .rearrange("(c z) -> c z", c=NCORES)
                 .rearrange("c (q s) -> c q s", s=320))

        V5_0 = ct["v5"]

        @blk.sync
        def _(s):
            dma(s, ldr, ctrashr[:, :], C["ctrash"])
            dma(s, ldr, cgid[:, :], C["gid"])
            s.wait_ge(cc, CC_IDX)
            for e in range(8):
                dma(s, ldr,
                    irep[16 * e:16 * (e + 1), :].rearrange("q (c s) -> q c s", c=8),
                    idxt3.rearrange("c q s -> q c s")[:, :, 0:NCOLL])
            s.dma_start(
                out=ctab[:, :],
                in_=idxt3[:, 0:8, NCOLL:NCOLL + 1].rearrange("c q s -> c (q s)"),
                allow_slow_non_contiguous=True).then_inc(ldr.h, 16)
            ldr.n += 1
        LDR_IREP = 16 * ldr.n

        @blk.vector
        def _(v):
            v.memset(onesw[:, :], 1.0)
            v.memset(zt[:, :], 0.0)
            v.drain()
            v.sem_inc(v5, 1)
            ct["v5"] += 1
            v.wait_ge(ldr.h, LDR_IREP)
            v.tensor_single_scalar(maskg[:, :], irep[:, :], cecon[:, :], ALU.is_equal)
            v.drain()
            v.sem_inc(v5, 1)
            ct["v5"] += 1
        V5_MASK = ct["v5"]

        @blk.tensor
        def _(t):
            t.wait_ge(v5, V5_MASK)
            for j in range(3):
                t.matmul(pb[j][:, :], cbtriu[:, :], maskg[:, j * 512:(j + 1) * 512],
                         start=True, stop=True)
            t.matmul(pb[3][:, :], cbtriu[:, :], maskg[:, 3 * 512:4 * 512],
                     start=True, stop=True).then_inc(t5, 1)
            ct["t5"] += 1
        T5_COLP = ct["t5"]

        @blk.vector
        def _(v):
            v.wait_ge(t5, T5_COLP)
            for j in range(4):
                v.tensor_copy(colp[:, j * 512:(j + 1) * 512], pb[j][:, :])
            v.drain()
            v.tensor_add(colsum[:, :], colp[:, :], maskg[:, :])
            v.drain()
            v.tensor_tensor_scan(cums[:, :], onesw[:, :], colsum[:, :], 0.0,
                                 ALU.mult, ALU.add)
            v.drain()
            v.tensor_sub(rex[:, :], cums[:, :], colsum[:, :])
            v.drain()
            v.sem_inc(v5, 1)
            ct["v5"] += 1
        V5_SCAN = ct["v5"]

        @blk.tensor
        def _(t):
            t.wait_ge(v5, V5_SCAN)
            for j in range(3):
                t.matmul(pb[j][:, :], cgspread[:, :], rex[:, j * 512:(j + 1) * 512],
                         start=True, stop=True)
            t.matmul(pb[3][:, :], cgspread[:, :], rex[:, 3 * 512:4 * 512],
                     start=True, stop=True).then_inc(t5, 1)
            ct["t5"] += 1
        T5_SPREAD = ct["t5"]

        @blk.vector
        def _(v):
            v.wait_ge(t5, T5_SPREAD)
            for j in range(4):
                v.tensor_copy(rex[:, j * 512:(j + 1) * 512], pb[j][:, :])
            v.drain()
            v.tensor_add(grank[:, :], colp[:, :], rex[:, :])
            v.drain()
            v.tensor_single_scalar(fullsel[:, :], grank[:, :], float(CAP), ALU.is_lt)
            v.drain()
            v.tensor_mul(fullsel[:, :], fullsel[:, :], maskg[:, :])
            v.drain()
            v.tensor_sub(cums[:, :], grank[:, :], ctrashr[:, :])
            v.drain()
            v.tensor_mul(fullsel[:, :], fullsel[:, :], cums[:, :])
            v.drain()
            v.tensor_add(fullsel[:, :], fullsel[:, :], ctrashr[:, :])
            v.drain()
            v.sem_inc(v5, 1)
            ct["v5"] += 1
        V5_FSEL = ct["v5"]

        @blk.tensor
        def _(t):
            t.wait_ge(v5, V5_FSEL)
            for j in range(3):
                t.matmul(pb[j][0:16, :], cgsel[:, :], fullsel[:, j * 512:(j + 1) * 512],
                         start=True, stop=True)
            t.matmul(pb[3][0:16, :], cgsel[:, :], fullsel[:, 3 * 512:4 * 512],
                     start=True, stop=True).then_inc(t5, 1)
            ct["t5"] += 1
        T5_GSEL = ct["t5"]

        @blk.vector
        def _(v):
            v.wait_ge(t5, T5_GSEL)
            for j in range(4):
                v.tensor_copy(i16f[:, j * 512:(j + 1) * 512], pb[j][0:16, :])
            v.drain()
            v.sem_inc(v5, 1)
            ct["v5"] += 1
        V5_I16F = ct["v5"]

        @blk.tensor
        def _(t):
            t.wait_ge(ldr.h, LDR_IREP)
            t.matmul(pb[5][0:8, 0:1], ctab[:, :], ccorelt[:, :],
                     start=True, stop=True).then_inc(t5, 1)
            ct["t5"] += 1
        T5_PC1 = ct["t5"]

        @blk.vector
        def _(v):
            v.wait_ge(t5, T5_PC1)
            v.tensor_copy(cnts[:, :], pb[5][0:8, 0:1])
            v.drain()
            v.sem_inc(v5, 1)
            ct["v5"] += 1
        V5_PC1 = ct["v5"]

        @blk.tensor
        def _(t):
            t.wait_ge(v5, V5_PC1)
            t.matmul(pb[5][:, 0:1], cgrp8[:, :], cnts[:, :],
                     start=True, stop=True).then_inc(t5, 1)
            ct["t5"] += 1
        T5_PC2 = ct["t5"]

        @blk.vector
        def _(v):
            v.wait_ge(t5, T5_PC2)
            v.tensor_copy(pcb[:, :], pb[5][:, 0:1])
            v.drain()
            v.tensor_add(grankl[:, :], colpl[:, :], rexl[:, :])
            v.drain()
            v.tensor_scalar_add(grankl[:, :], grankl[:, :], pcb[:, :])
            v.drain()
            v.tensor_single_scalar(kml[:, :], grankl[:, :], float(CAP), ALU.is_lt)
            v.drain()
            v.tensor_mul(kml[:, :], kml[:, :], maskl[:, :])
            v.drain()
            v.tensor_mul(grankl[:, :], grankl[:, :], maskl[:, :])
            v.drain()
            v.sem_inc(v5, 1)
            ct["v5"] += 1
        V5_OWN = ct["v5"]

        @blk.tensor
        def _(t):
            t.wait_ge(v5, V5_OWN)
            t.matmul(pb[4][0:16, 0:NCOLL], cgsum16[:, :], kml[:, :],
                     start=True, stop=True)
            t.matmul(pb[5][0:16, 0:NCOLL], cgsum16[:, :], grankl[:, :],
                     start=True, stop=True)
            t.matmul(pb[6][0:16, 0:NCOLL], cgsumE[:, :], maskl[:, :],
                     start=True, stop=True).then_inc(t5, 1)
            ct["t5"] += 1
        T5_SEL = ct["t5"]

        @blk.vector
        def _(v):
            v.wait_ge(t5, T5_SEL)
            v.tensor_copy(t16a[:, :], pb[4][0:16, 0:NCOLL])
            v.tensor_copy(t16b[:, :], pb[5][0:16, 0:NCOLL])
            v.tensor_copy(g16[:, :], pb[6][0:16, 0:NCOLL])
            v.drain()
            v.tensor_single_scalar(s16[:, :], t16b[:, :], 640.0, ALU.is_ge)
            for m in range(2, 8):
                v.drain()
                v.tensor_single_scalar(t16a2[:, :], t16b[:, :], 640.0 * m, ALU.is_ge)
                v.drain()
                v.tensor_add(s16[:, :], s16[:, :], t16a2[:, :])
            v.drain()
            v.tensor_single_scalar(t16a2[:, :], t16b[:, :], 4480.0, ALU.is_ge)
            v.drain()
            v.tensor_single_scalar(il16[0:16, 0:NCOLL], t16b[:, :], 4864.0, ALU.is_ge)
            v.drain()
            v.scalar_tensor_tensor(t16b[:, :], s16[:, :], 4480.0, t16b[:, :],
                                   ALU.mult, ALU.add)
            v.drain()
            v.scalar_tensor_tensor(t16b[:, :], il16[0:16, 0:NCOLL], 2688.0,
                                   t16b[:, :], ALU.mult, ALU.add)
            v.drain()
            v.tensor_single_scalar(s16[:, :], t16a2[:, :], -256.0, ALU.mult)
            v.drain()
            v.tensor_single_scalar(s16[:, :], s16[:, :], 640.0, ALU.add)
            v.drain()
            v.scalar_tensor_tensor(s16[:, :], il16[0:16, 0:NCOLL], -128.0,
                                   s16[:, :], ALU.mult, ALU.add)
            v.drain()
            v.tensor_scalar_mul(g16[:, :], g16[:, :], 1.0 / 5120.0)
            v.drain()
            v.tensor_mul(g16[:, :], g16[:, :], s16[:, :])
            v.drain()
            v.tensor_add(g16[:, :], g16[:, :], t16b[:, :])
            v.drain()
            v.tensor_mul(g16[:, :], g16[:, :], t16a[:, :])
            v.tensor_mul(s16[:, :], tp16[:, :], t16a[:, :])
            v.drain()
            v.sem_inc(v5, 1)
            ct["v5"] += 1
        V5_G16 = ct["v5"]

        @blk.tensor
        def _(t):
            t.wait_ge(v5, V5_G16)
            for j, src in enumerate([g16, s16]):
                for sh in range(2):
                    ins = t.transpose(pb[4 + 2 * j + sh][0:128, 0:16],
                                src[:, sh * 128:(sh + 1) * 128],
                                cident[0:16, 0:16])
            ins.then_inc(t5, 1)
            ct["t5"] += 1
        T5_GT = ct["t5"]

        @blk.vector
        def _(v):
            v.wait_ge(t5, T5_GT)
            for sh in range(2):
                v.tensor_copy(gcols[:, sh * 16:(sh + 1) * 16], pb[4 + sh][0:128, 0:16])
                v.tensor_copy(scols[:, sh * 16:(sh + 1) * 16], pb[6 + sh][0:128, 0:16])
            v.drain()
            v.sem_inc(v5, 1)
            ct["v5"] += 1

        if stage == "routemath":
            @blk.scalar
            def _(s):
                s.wait_ge(v5, ct["v5"])
                dma(s, sto, out[0:128, :], fullsel[:, 0:1024])
                dma(s, sto, out[128:256, :], fullsel[:, 1024:2048])
                dma(s, sto, out[256:384, 0:NCOLL], grankl[:, :])
                dma(s, sto, out[384:400, 0:NCOLL], g16[:, :])
                dma(s, sto, out[400:416, 0:NCOLL], s16[:, :])
                dma(s, sto, out[416:544, 0:32], gcols.bitcast(F32)[:, :])
                dma(s, sto, out[544:672, 0:32], scols[:, :])
                dma(s, sto, out[672:688, 0:1024], i16f.bitcast(F32)[:, :])
                s.wait_ge(sto.h, 16 * sto.n)
            return nc

        @blk.sync
        def _(s):
            s.wait_ge(v5, V5_I16F)
            for e in range(8):
                dma(s, ldz, i16rep[16 * e:16 * (e + 1), :], i16f[:, :])
        LDZ_T = 16 * ldz.n

        @blk.gpsimd
        def _(g):
            g.wait_ge(ldz.h, LDZ_T)
            g.wait_ge(qpz.h, 16 * qpz.n)
            g.wait_ge(ldr.h, LDR_IREP)
            for k in range(8):
                g.dma_scatter_add(
                    idtab[k * (CAP + 1024):(k + 1) * (CAP + 1024), 0:1],
                    cgid[:, 32 * k:32 * (k + 1)].rearrange("p (n one) -> p n one", one=1),
                    i16rep[:, 256 * k:256 * (k + 1)],
                    num_idxs=TG // 8, num_idxs_reg=TG // 8, elem_size=1, elem_step=64,
                ).then_inc(qsc[k].h, 16)
                qsc[k].n += 1

        V5_IDW0 = ct["v5"]

        @blk.scalar
        def _(s):
            for k in range(8):
                s.wait_ge(qsc[k].h, 16 * qsc[k].n)
                if k >= 2:
                    s.wait_ge(v5, V5_IDW0 + k - 1)
                dma(s, qid[k % 2],
                    idwb[k % 2][:, :].rearrange("p (a c) -> p a c", c=64),
                    idtab[k * (CAP + 1024):(k + 1) * (CAP + 1024), :]
                    .rearrange("(a p) c -> p a c", p=128)[:, 0:NST, :])

        @blk.vector
        def _(v):
            for k in range(8):
                v.wait_ge(qid[k % 2].h, 16 * (k // 2 + 1))
                if k == 0:
                    v.tensor_copy(idw[:, 0:NST * 64:64],
                                  idwb[0][:, 0:NST * 64:64]).then_inc(v5, 1)
                else:
                    v.tensor_add(idw[:, 0:NST * 64:64], idw[:, 0:NST * 64:64],
                                 idwb[k % 2][:, 0:NST * 64:64]).then_inc(v5, 1)
                v.drain()
                ct["v5"] += 1
        V5_IDW8 = ct["v5"]

        @blk.vector
        def _(v):
            v.wait_ge(v5, V5_IDW8)
            v.tensor_single_scalar(zt[:, 0:NST], idw[:, 0:NST * 64:64], 1.0, ALU.subtract)
            v.drain()
            v.tensor_single_scalar(zt[:, 0:NST], zt[:, 0:NST], 0.0, ALU.max)
            v.drain()
            v.tensor_single_scalar(zt[:, NST:2 * NST], zt[:, 0:NST], 4096.0, ALU.is_ge)
            for m in range(2, 8):
                v.drain()
                v.tensor_single_scalar(zt[:, 3 * NST:4 * NST], zt[:, 0:NST],
                                       4096.0 * m, ALU.is_ge)
                v.drain()
                v.tensor_add(zt[:, NST:2 * NST], zt[:, NST:2 * NST],
                             zt[:, 3 * NST:4 * NST])
            v.drain()
            v.scalar_tensor_tensor(zt[:, 2 * NST:3 * NST], zt[:, NST:2 * NST],
                                   -4096.0, zt[:, 0:NST], ALU.mult, ALU.add)
            v.drain()
            v.tensor_single_scalar(zt[:, 0:NST], zt[:, 2 * NST:3 * NST], 1024.0, ALU.is_ge)
            for m in range(2, 4):
                v.drain()
                v.tensor_single_scalar(zt[:, 3 * NST:4 * NST], zt[:, 2 * NST:3 * NST],
                                       1024.0 * m, ALU.is_ge)
                v.drain()
                v.tensor_add(zt[:, 0:NST], zt[:, 0:NST], zt[:, 3 * NST:4 * NST])
            v.drain()
            v.scalar_tensor_tensor(zt[:, 2 * NST:3 * NST], zt[:, 0:NST],
                                   7168.0, zt[:, 2 * NST:3 * NST], ALU.mult, ALU.add)
            v.drain()
            v.scalar_tensor_tensor(zt[:, 2 * NST:3 * NST], zt[:, NST:2 * NST],
                                   1024.0, zt[:, 2 * NST:3 * NST], ALU.mult, ALU.add)
            v.drain()
            v.tensor_copy(idcols[:, :], zt[:, 2 * NST:3 * NST])
            v.drain()
            v.sem_inc(v5, 1)
            ct["v5"] += 1
        V5_IDC = ct["v5"]

        barrier()

        regB.close()

        regC = contextlib.ExitStack()
        es.enter_context(regC)
        rct = lambda name, shape, dtt=F32: regC.enter_context(
            nc.sbuf_tensor(name, shape, dtt))
        w1sb = rct("w1sb", [128, DC * FBS])
        w2sb = rct("w2sb", [128, FCB * D])
        y1sb = rct("y1sb", [128, FCB * TSL])
        sfb = [rct(f"sfb{k}", [128, DC * TSL]) for k in range(2)]
        y2ev = [rct(f"y2ev{k}", [128, D]) for k in range(2)]
        y2pl = [rct(f"y2pl{k}", [128, D]) for k in range(2)]
        regP6 = contextlib.ExitStack()
        es.enter_context(regP6)
        slt = [regP6.enter_context(nc.sbuf_tensor(f"slt{k}", [128, D], F32))
               for k in range(2)]
        sle = [regP6.enter_context(nc.sbuf_tensor(f"sle{k}", [128, D], F32))
               for k in range(2)]

        TN_P6 = ct["tn"]

        @blk.gpsimd
        def _(g):
            g.wait_ge(cc, CC_H2)
            g.wait_ge(v5, V5_IDC)
            for n in range(NST):
                if n >= 2:
                    g.wait_ge(tn, TN_P6 + 8 * (n - 1))
                g.indirect_dma_start(
                    out=slt[n % 2][:, :],
                    out_offset=None,
                    in_=h2glob[:, :],
                    in_offset=IndirectOffsetOnAxis(ap=idcols[:, n:n + 1], axis=0),
                ).then_inc(qsl[n % 2].h, 16)
                qsl[n % 2].n += 1

        @blk.sync
        def _(s):
            for dcx in range(DC):
                dma(s, ldwb, w1sb[:, dcx * FBS:(dcx + 1) * FBS],
                    w1[dcx * 128:(dcx + 1) * 128, 0:FBS])
            for fc in range(FCB):
                dma(s, ldwb, w2sb[:, fc * D:(fc + 1) * D],
                    w2[fc * 128:(fc + 1) * 128, :])
        LDWB_0 = 16 * ldwb.n

        @blk.tensor
        def _(t):
            for n in range(NST):
                t.wait_ge(qsl[n % 2].h, 16 * (n // 2 + 1))
                for d in range(DC):
                    k = 8 * n + d
                    if k >= 2:
                        t.wait_ge(evB0 if k % 2 == 0 else evB1, k // 2)
                    t.transpose(pb[k % 2][:, 0:128],
                                slt[n % 2][:, d * 128:(d + 1) * 128],
                                cident[:, :]).then_inc(tn, 1)
        ct["tn"] += 8 * NST

        @blk.vector
        def _(v):
            for n in range(NST):
                for d in range(0, DC, 2):
                    k = 8 * n + d
                    v.wait_ge(tn, TN_P6 + k + 1)
                    if n >= 2 and d == 0:
                        v.wait_ge(stf[n % 2].h, 16 * 8 * (n // 2))
                    v.tensor_copy(sle[n % 2][:, d * 128:(d + 1) * 128],
                                  pb[k % 2][:, 0:128]).then_inc(evB0, 1)

        @blk.scalar
        def _(s):
            for n in range(NST):
                for d in range(1, DC, 2):
                    k = 8 * n + d
                    s.wait_ge(tn, TN_P6 + k + 1)
                    if n >= 2 and d == 1:
                        s.wait_ge(stf[n % 2].h, 16 * 8 * (n // 2))
                    s.activation(sle[n % 2][:, d * 128:(d + 1) * 128],
                                 pb[k % 2][:, 0:128], AF.Copy,
                                 bias=0.0, scale=1.0).then_inc(evB1, 1)
                s.wait_ge(evB0, 4 * (n + 1))
                s.wait_ge(evB1, 4 * (n + 1))
                for d in range(DC):
                    dma(s, stf[n % 2],
                        slabf[d * 128:(d + 1) * 128, n * 128:(n + 1) * 128],
                        sle[n % 2][:, d * 128:(d + 1) * 128])

        for fb in range(FBLK):
            if fb == 1:
                @blk.sync
                def _(s):
                    for dcx in range(DC):
                        dma(s, ldwb, w1sb[:, dcx * FBS:(dcx + 1) * FBS],
                            w1[dcx * 128:(dcx + 1) * 128, FBS:FF])
                    for fc in range(FCB):
                        dma(s, ldwb, w2sb[:, fc * D:(fc + 1) * D],
                            w2[FBS + fc * 128:FBS + (fc + 1) * 128, :])

            LDWB_FB = 16 * ldwb.n
            TN_FB = ct["tn"]

            @blk.sync
            def _(s):
                for ts_ in range(NSL):
                    if ts_ >= 2:
                        s.wait_ge(tn, TN_FB + 20 * (ts_ - 1))
                    if fb == 0:
                        ntile = 2 * ts_ + 2
                        s.wait_ge(stf[0].h, 16 * 8 * ((ntile + 1) // 2))
                        s.wait_ge(stf[1].h, 16 * 8 * (ntile // 2))
                    for d in range(DC):
                        dma(s, lsf[ts_ % 2], sfb[ts_ % 2][:, d * TSL:(d + 1) * TSL],
                            slabf[d * 128:(d + 1) * 128, ts_ * TSL:(ts_ + 1) * TSL])
                    if fb == 1:
                        for tc in range(2):
                            if ts_ >= 1:
                                s.wait_ge(evD, (NSL + ts_ - 1) * 4 + 2 * tc + 2)
                            dma(s, lyp[tc % 2], y2pl[tc % 2][:, :],
                                y2partial[ts_ * TSL + tc * 128:
                                          ts_ * TSL + (tc + 1) * 128, :])

            @blk.tensor
            def _(t):
                t.wait_ge(ldwb.h, LDWB_FB)
                for ts_ in range(NSL):
                    gs = fb * NSL + ts_
                    t.wait_ge(lsf[ts_ % 2].h, 16 * DC * (gs // 2 + 1))
                    for fc in range(FCB):
                        bank = 2 + (fc % 2)
                        gfc = (fb * NSL + ts_) * FCB + fc
                        if gfc >= 2:
                            t.wait_ge(evC, gfc - 1)
                        for dcx in range(DC):
                            ins = t.matmul(pb[bank][:, 0:TSL],
                                     w1sb[:, dcx * FBS + fc * 128:dcx * FBS + (fc + 1) * 128],
                                     sfb[ts_ % 2][:, dcx * TSL:(dcx + 1) * TSL],
                                     start=(dcx == 0), stop=(dcx == DC - 1))
                        ins.then_inc(tn, 1)
                    t.wait_ge(evC, (fb * NSL + ts_) * FCB + FCB)
                    for tc in range(2):
                        for dh in range(2):
                            bank = 4 + (2 * tc + dh) % 2
                            gi = (fb * NSL + ts_) * 4 + 2 * tc + dh
                            if gi >= 2:
                                t.wait_ge(evD, gi - 1)
                            for fc in range(FCB):
                                ins = t.matmul(pb[bank][:, :],
                                         y1sb[:, fc * TSL + tc * 128:fc * TSL + (tc + 1) * 128],
                                         w2sb[:, fc * D + dh * 512:fc * D + dh * 512 + 512],
                                         start=(fc == 0),
                                         stop=(fb == 1 and fc == FCB - 1))
                            if fb == 0:
                                ins = t.matmul(pb[bank][:, :], conesr[:, :],
                                         sb2r[:, dh * 512:dh * 512 + 512],
                                         start=False, stop=True)
                            ins.then_inc(tn, 1)
            ct["tn"] += 20 * NSL

            @blk.scalar
            def _(s):
                tgt = y2partial if fb == 0 else y2slab
                for ts_ in range(NSL):
                    for fc in range(FCB):
                        s.wait_ge(tn, TN_FB + 20 * ts_ + fc + 1)
                        s.activation(y1sb[:, fc * TSL:(fc + 1) * TSL],
                                     pb[2 + (fc % 2)][:, 0:TSL], AF.Relu,
                                     bias=sb1t[:, fb * FCB + fc:fb * FCB + fc + 1],
                                     scale=1.0).then_inc(evC, 1)
                    for tc in range(2):
                        s.wait_ge(evD, (fb * NSL + ts_) * 4 + 2 * tc + 2)
                        dma(s, sty[tc % 2],
                            tgt[ts_ * TSL + tc * 128:ts_ * TSL + (tc + 1) * 128, :],
                            y2ev[tc % 2][:, :])

            @blk.vector
            def _(v):
                for ts_ in range(NSL):
                    gs = fb * NSL + ts_
                    for tc in range(2):
                        if gs >= 1:
                            v.wait_ge(sty[tc % 2].h, 16 * gs)
                        for dh in range(2):
                            v.wait_ge(tn, TN_FB + 20 * ts_ + FCB + 2 * tc + dh + 1)
                            bank = 4 + (2 * tc + dh) % 2
                            if fb == 0:
                                v.tensor_copy(y2ev[tc % 2][:, dh * 512:dh * 512 + 512],
                                              pb[bank][:, :]).then_inc(evD, 1)
                            else:
                                v.wait_ge(lyp[tc % 2].h, 16 * (ts_ + 1))
                                v.tensor_add(y2ev[tc % 2][:, dh * 512:dh * 512 + 512],
                                             y2pl[tc % 2][:, dh * 512:dh * 512 + 512],
                                             pb[bank][:, :]).then_inc(evD, 1)

            if fb == 1:
                YB = [0, 640, 1280, 1920, 2560, 3200, 3840, 4480, 4864, 5120]

                @blk.gpsimd
                def _(g):
                    for k in range(9):
                        nslice = (YB[k + 1] + TSL - 1) // TSL
                        g.wait_ge(sty[0].h, 16 * (NSL + nslice))
                        g.wait_ge(sty[1].h, 16 * (NSL + nslice))
                        g.collective_compute(
                            "AllGather", ALU.bypass,
                            replica_groups=[list(range(NCORES))],
                            ins=[y2slab[YB[k]:YB[k + 1], :].opt()],
                            outs=[y2glob[8 * YB[k]:8 * YB[k + 1], :].opt()],
                        ).then_inc(cc, 1)
                ct["cc"] += 9
                CC_Y2 = ct["cc"]

            barrier()

        regP6.close()
        regC.close()


        regD = contextlib.ExitStack()
        es.enter_context(regD)
        gout = [regD.enter_context(nc.sbuf_tensor(f"gout{k}", [128, D], F32))
                for k in range(4)]
        x2b = [regD.enter_context(nc.sbuf_tensor(f"x2b{k}", [128, D], F32))
               for k in range(4)]
        otb = [regD.enter_context(nc.sbuf_tensor(f"otb{k}", [128, D], F32))
               for k in range(4)]

        x2v = x2d.rearrange("(r s) d -> r s d", s=16)
        outv = out.rearrange("(r s) d -> r s d", s=16)

        @blk.sync
        def _(s):
            for gi in range(32):
                if gi >= 4:
                    s.wait_ge(vO, gi - 3)
                q, sh = gi % 16, gi // 16
                dma(s, lx4[gi % 4], x2b[gi % 4][:, :],
                    x2v[sh * 128:(sh + 1) * 128, q, :])

        @blk.gpsimd
        def _(g):
            g.wait_ge(cc, CC_Y2)
            for gi in range(32):
                if gi >= 4:
                    g.wait_ge(vO, gi - 3)
                q, sh = gi % 16, gi // 16
                g.indirect_dma_start(
                    out=gout[gi % 4][:, :],
                    out_offset=None,
                    in_=y2glob[:, :],
                    in_offset=IndirectOffsetOnAxis(
                        ap=gcols[:, sh * 16 + q:sh * 16 + q + 1], axis=0),
                ).then_inc(qg4[gi % 4].h, 16)
                qg4[gi % 4].n += 1

        @blk.vector
        def _(v):
            for gi in range(32):
                q, sh = gi % 16, gi // 16
                v.wait_ge(lx4[gi % 4].h, 16 * (gi // 4 + 1))
                v.wait_ge(qg4[gi % 4].h, 16 * (gi // 4 + 1))
                if gi >= 4:
                    v.wait_ge(sp4[gi % 4].h, 16 * (gi // 4))
                v.scalar_tensor_tensor(otb[gi % 4][:, :], gout[gi % 4][:, :],
                                       scols[:, sh * 16 + q:sh * 16 + q + 1],
                                       x2b[gi % 4][:, :],
                                       ALU.mult, ALU.add).then_inc(vO, 1)

        @blk.scalar
        def _(s):
            for gi in range(32):
                q, sh = gi % 16, gi // 16
                s.wait_ge(vO, gi + 1)
                dma(s, sp4[gi % 4], outv[sh * 128:(sh + 1) * 128, q, :],
                    otb[gi % 4][:, :])
            for k in range(4):
                s.wait_ge(sp4[k].h, 16 * sp4[k].n)

    return nc


_NC_CACHE = {}


def _get_nc():
    if _STAGE not in _NC_CACHE:
        nc = _build(_STAGE)
        nc.finalize()
        _NC_CACHE[_STAGE] = nc
    return _NC_CACHE[_STAGE]


def kernel(L0_attn_hidden_states, L0_attn_W_q, L0_ffn_gate_w, L0_ffn_gate_b,
           L0_ffn_w1, L0_ffn_b1, L0_ffn_w2, L0_ffn_b2):
    x = np.asarray(L0_attn_hidden_states, dtype=np.float32)
    wqv = np.ascontiguousarray(np.asarray(L0_attn_W_q, dtype=np.float32))
    gw = np.asarray(L0_ffn_gate_w, dtype=np.float32)
    gbv = np.asarray(L0_ffn_gate_b, dtype=np.float32)
    w1v = np.asarray(L0_ffn_w1, dtype=np.float32)
    b1v = np.asarray(L0_ffn_b1, dtype=np.float32)
    w2v = np.asarray(L0_ffn_w2, dtype=np.float32)
    b2v = np.asarray(L0_ffn_b2, dtype=np.float32)

    consts = _host_consts()
    gwtv = np.ascontiguousarray(gw.T)
    gbr = np.ascontiguousarray(gbv.reshape(1, E))
    in_maps = []
    for c in range(NCORES):
        gsel = np.zeros((128, 16), np.float32)
        for q in range(16):
            gsel[16 * c + q, q] = 1.0
        clt = np.zeros((8, 1), np.float32)
        clt[:c, 0] = 1.0
        m = {
            "x": np.ascontiguousarray(x[c]),
            "wq": wqv,
            "gwt": gwtv,
            "gb": gbr,
            "w1": np.ascontiguousarray(w1v[c]),
            "b1t": np.ascontiguousarray(b1v[c].reshape(FF // 128, 128).T),
            "w2": np.ascontiguousarray(w2v[c]),
            "b2r": np.ascontiguousarray(b2v[c].reshape(1, D)),
            "gsel": gsel,
            "corelt": clt,
        }
        m.update(consts)
        in_maps.append(m)

    nc = _get_nc()
    res = run_bass_kernel_spmd(nc, in_maps, core_ids=list(range(NCORES)))
    outp = np.stack([np.asarray(r["out"]) for r in res.results], axis=0)
    return outp.astype(np.float32)


# version 18
# speedup vs baseline: 1.1007x; 1.0399x over previous
import contextlib
import numpy as np

import concourse.bacc as bacc
import concourse.mybir as mybir
from concourse.bass_utils import run_bass_kernel_spmd
from concourse.bass_core_rust import IndirectOffsetOnAxis

F32 = mybir.dt.float32
I16 = mybir.dt.int16
U32 = mybir.dt.uint32
AF = mybir.ActivationFunctionType
ALU = mybir.AluOpType
AX = mybir.AxisListType

B, S, D = 8, 4096, 1024
NCORES = 8
TL = S
NT = TL // 128
DC = D // 128
E = 8
FF = 4096
CAP = 5120
TG = B * S
NCOL = TG // 16
NCOLL = TL // 16
EPS = 1e-5
TRASH = CAP
FBLK = 2
FBS = FF // FBLK
FCB = FBS // 128
TSL = 256
NSL = CAP // TSL
NST = CAP // 128
NSC = 32

_STAGE = "full"


def _host_consts():
    ident = np.eye(128, dtype=np.float32)
    btriu = np.zeros((128, 128), np.float32)
    for e in range(8):
        for q2 in range(16):
            for q in range(16):
                if q2 < q:
                    btriu[16 * e + q2, 16 * e + q] = 1.0
    gspread = np.zeros((128, 128), np.float32)
    for m in range(128):
        gspread[16 * (m // 16) + 15, m] = 1.0
    gsum16 = np.zeros((128, 16), np.float32)
    gsumE = np.zeros((128, 16), np.float32)
    for e in range(8):
        for q in range(16):
            gsum16[16 * e + q, q] = 1.0
            gsumE[16 * e + q, q] = float(CAP * e)
    sel15 = np.zeros((128, 8), np.float32)
    for e in range(8):
        sel15[16 * e + 15, e] = 1.0
    grp8 = np.zeros((8, 128), np.float32)
    for e in range(8):
        grp8[e, 16 * e:16 * (e + 1)] = 1.0
    econst = (np.arange(128, dtype=np.float32) // 16).reshape(128, 1)
    gid = np.zeros((128, 256), np.float32)
    for i in range(TG // NCORES // 16 * 128 if False else 32768):
        gid[i % 128, i // 128] = float(i + 1)
    ctrash = np.zeros((128, 2048), np.float32)
    for p in range(128):
        q = p % 16
        for j in range(2048):
            ctrash[p, j] = float(CAP + ((16 * j + q) % 1024))
    onesr = np.ones((1, 128), np.float32)
    epsv = np.full((128, 1), EPS, np.float32)
    return dict(ident=ident, btriu=btriu, gspread=gspread, gsum16=gsum16,
                gsumE=gsumE, sel15=sel15, grp8=grp8, econst=econst,
                onesr=onesr, epsv=epsv, gid=gid, ctrash=ctrash,
                zeros=np.zeros((128, 1024), np.float32))


def _build(stage):
    nc = bacc.Bacc("TRN2", target_bir_lowering=False)
    dt = nc.dram_tensor

    x_in = dt("x", [TL, D], F32, kind="ExternalInput").ap()
    wq = dt("wq", [D, D], F32, kind="ExternalInput").ap()
    gwt = dt("gwt", [D, E], F32, kind="ExternalInput").ap()
    gb = dt("gb", [1, E], F32, kind="ExternalInput").ap()
    w1 = dt("w1", [D, FF], F32, kind="ExternalInput").ap()
    b1t = dt("b1t", [128, FF // 128], F32, kind="ExternalInput").ap()
    w2 = dt("w2", [FF, D], F32, kind="ExternalInput").ap()
    b2r = dt("b2r", [1, D], F32, kind="ExternalInput").ap()
    gsel = dt("gsel", [128, 16], F32, kind="ExternalInput").ap()
    corelt = dt("corelt", [8, 1], F32, kind="ExternalInput").ap()
    C = {k: dt(k, list(v.shape), F32, kind="ExternalInput").ap()
         for k, v in _host_consts().items()}
    out = dt("out", [TL, D], F32, kind="ExternalOutput").ap()

    x2d = dt("x2d", [TL, D], F32, kind="Internal").ap()
    h2shard = dt("h2shard", [TL + 5, D], F32, kind="Internal").ap()
    h2glob = dt("h2glob", [TG, D], F32, kind="Internal").ap()
    idxg2 = dt("idxg2", [NCORES * 5, D], F32, kind="Internal").ap()
    cnt8 = dt("cnt8", [8, 1], F32, kind="Internal").ap()
    cnttab = dt("cnttab", [8, 8], F32, kind="Internal").ap()
    idtab = dt("idtab", [8 * (CAP + 1024), 64], F32, kind="Internal").ap()
    slabf = dt("slabf", [D, CAP], F32, kind="Internal").ap()
    y2partial = dt("y2partial", [CAP, D], F32, kind="Internal").ap()
    y2slab = dt("y2slab", [CAP, D], F32, kind="Internal").ap()
    y2glob = dt("y2glob", [NCORES * CAP, D], F32, kind="Internal").ap()

    es = contextlib.ExitStack()
    with es:
        sbt = lambda name, shape, dtt=F32: es.enter_context(
            nc.sbuf_tensor(name, shape, dtt))
        sem = lambda name: es.enter_context(nc.semaphore(name))
        pb = [es.enter_context(nc.psum_tensor(f"pb{k}", [128, 512], F32))
              for k in range(8)]

        cident = sbt("cident", [128, 128])
        cbtriu = sbt("cbtriu", [128, 128])
        cgspread = sbt("cgspread", [128, 128])
        cgsum16 = sbt("cgsum16", [128, 16])
        cgsumE = sbt("cgsumE", [128, 16])
        csel15 = sbt("csel15", [128, 8])
        cgrp8 = sbt("cgrp8", [8, 128])
        cecon = sbt("cecon", [128, 1])
        conesr = sbt("conesr", [1, 128])
        cepsv = sbt("cepsv", [128, 1])
        cgsel = sbt("cgsel", [128, 16])
        idcols = sbt("idcols", [128, NST], U32)
        ccorelt = sbt("ccorelt", [8, 1])
        sgwt = sbt("sgwt", [128, DC * E])
        sgb = sbt("sgb", [1, E])
        sb2r = sbt("sb2r", [1, D])
        sb1t = sbt("sb1t", [128, FF // 128])
        mcol = sbt("mcol", [128, NT])
        rcol = sbt("rcol", [128, NT])
        ncol = sbt("ncol", [128, NT])
        tcol = sbt("tcol", [128, NT])
        idxl = sbt("idxl", [128, NT])
        tpl = sbt("tpl", [128, NT])
        il16 = sbt("il16", [16, 320])
        tp16 = sbt("tp16", [16, NCOLL])
        g16 = sbt("g16", [16, NCOLL])
        s16 = sbt("s16", [16, NCOLL])
        gcols = sbt("gcols", [128, 32], U32)
        maskl = sbt("maskl", [128, NCOLL])
        colpl = sbt("colpl", [128, NCOLL])
        rexl = sbt("rexl", [128, NCOLL])
        cnts = sbt("cnts", [8, 1])
        scols = sbt("scols", [128, 32])

        cc = sem("cc")
        tn = sem("tn"); bar = sem("bar")
        vst = sem("vst"); ssq = sem("ssq"); vrs = sem("vrs"); sh1 = sem("sh1")
        evA = sem("evA"); evH = sem("evH"); evF = sem("evF")
        vX2 = sem("vX2"); vSm = sem("vSm"); sEx = sem("sEx")
        v5 = sem("v5"); t5 = sem("t5")
        evB0 = sem("evB0"); evB1 = sem("evB1")
        evC = sem("evC"); evD = sem("evD"); vO = sem("vO")

        class Q:
            def __init__(self, name, eng):
                self.h = sem(name)
                self.n = 0
                self.eng = eng
        QREG = []
        def mkq(name, eng):
            q = Q(name, eng); QREG.append(q); return q
        def dma(engapi, q, out_ap, in_ap):
            engapi.dma_start(out=out_ap, in_=in_ap).then_inc(q.h, 16)
            q.n += 1
            return q.n
        ldc = mkq("ldc", "sync")
        ldx = [mkq(f"ldx{k}", "sync") for k in range(2)]
        ldw = [mkq(f"ldw{k}", "sync") for k in range(2)]
        ldr = mkq("ldr", "sync")
        ldz = mkq("ldz", "sync")
        lds = [mkq(f"lds{k}", "sync") for k in range(2)]
        lsl = [mkq(f"lsl{k}", "sync") for k in range(2)]
        ldwb = mkq("ldwb", "sync")
        lsf = [mkq(f"lsf{k}", "sync") for k in range(2)]
        lyp = [mkq(f"lyp{k}", "sync") for k in range(2)]
        lx2 = [mkq(f"lx2{k}", "sync") for k in range(2)]
        stx = mkq("stx", "scalar")
        smh = mkq("smh", "scalar")
        stf = [mkq(f"stf{k}", "scalar") for k in range(2)]
        sty = [mkq(f"sty{k}", "scalar") for k in range(2)]
        sid = mkq("sid", "scalar")
        scn = mkq("scn", "scalar")
        sto = mkq("sto", "scalar")
        sop = [mkq(f"sop{k}", "scalar") for k in range(2)]
        qsc = [mkq(f"qsc{k}", "gpsimd") for k in range(8)]
        qid = [mkq(f"qid{k}", "sync") for k in range(2)]
        qpz = mkq("qpz", "scalar")
        qsl = [mkq(f"qsl{k}", "gpsimd") for k in range(2)]
        qg4 = [mkq(f"qg4_{k}", "gpsimd") for k in range(4)]
        lx4 = [mkq(f"lx4_{k}", "sync") for k in range(4)]
        sp4 = [mkq(f"sp4_{k}", "scalar") for k in range(4)]
        qgi = [mkq(f"qgi{k}", "gpsimd") for k in range(2)]

        ct = {"cc": 0, "tn": 0, "t5": 0, "v5": 0}
        NBAR = [0]
        blk = es.enter_context(nc.Block())

        def barrier():
            NBAR[0] += 1
            tgt = 5 * NBAR[0]
            snap = [(q, q.n) for q in QREG if q.n > 0]

            @blk.sync
            def _(s):
                for q, n in snap:
                    if q.eng == "sync":
                        s.wait_ge(q.h, 16 * n)
                s.sem_inc(bar, 1)
                s.wait_ge(bar, tgt)

            @blk.tensor
            def _(t):
                t.drain()
                t.sem_inc(bar, 1)
                t.wait_ge(bar, tgt)

            @blk.vector
            def _(v):
                v.drain()
                v.sem_inc(bar, 1)
                v.wait_ge(bar, tgt)

            @blk.scalar
            def _(s):
                s.drain()
                for q, n in snap:
                    if q.eng == "scalar":
                        s.wait_ge(q.h, 16 * n)
                s.sem_inc(bar, 1)
                s.wait_ge(bar, tgt)

            @blk.gpsimd
            def _(g):
                for q, n in snap:
                    if q.eng == "gpsimd":
                        g.wait_ge(q.h, 16 * n)
                g.sem_inc(bar, 1)
                g.wait_ge(bar, tgt)

        @blk.sync
        def _(s):
            for apto, apfrom in [
                (cident, C["ident"]), (cbtriu, C["btriu"]),
                (cgspread, C["gspread"]), (cgsum16, C["gsum16"]),
                (cgsumE, C["gsumE"]), (csel15, C["sel15"]),
                (cgrp8, C["grp8"]), (cecon, C["econst"]),
                (conesr, C["onesr"]), (cepsv, C["epsv"]),
                (cgsel, gsel), (ccorelt, corelt), (sgb, gb),
                (sb2r, b2r), (sb1t, b1t),
            ]:
                dma(s, ldc, apto[:, :], apfrom)
            for k in range(DC):
                dma(s, ldc, sgwt[:, k * E:(k + 1) * E],
                    gwt[k * 128:(k + 1) * 128, :])
        flatz = idtab.rearrange("r c -> (r c)")
        @blk.scalar
        def _(s):
            for k in range(8):
                base = k * (CAP + 1024) * 64
                for n in range(3):
                    dma(s, qpz,
                        flatz[base + n * 131072:base + (n + 1) * 131072]
                        .rearrange("(p c) -> p c", p=128),
                        C["zeros"])
        LDC_T = 16 * ldc.n

        regA = contextlib.ExitStack()
        es.enter_context(regA)
        ra = lambda name, shape, dtt=F32: regA.enter_context(
            nc.sbuf_tensor(name, shape, dtt))
        h1 = ra("h1", [128, NT * D])
        MG = ra("MG", [128, DC * D])
        xt = [ra(f"xt{k}", [128, D]) for k in range(2)]
        x2t = ra("x2t", [128, D])
        h2t = ra("h2t", [128, D])
        sqt = ra("sqt", [128, D])
        h1T = [ra(f"h1T{k}", [128, D]) for k in range(2)]
        h2T = ra("h2T", [128, D])
        wqb = [sqt[:, k * 128:(k + 1) * 128] for k in range(2)]
        lgt = ra("lgt", [128, 16])
        aidx = ra("aidx", [128, 8], U32)

        @blk.sync
        def _(s):
            for i in range(NT):
                if i >= 2:
                    s.wait_ge(sh1, i - 1)
                dma(s, ldx[i % 2], xt[i % 2][:, :],
                    x_in[i * 128:(i + 1) * 128, :])

        @blk.vector
        def _(v):
            for i in range(NT):
                v.wait_ge(ldx[i % 2].h, 16 * (i // 2 + 1))
                v.reduce_sum(tcol[:, i:i + 1], xt[i % 2][:, :], axis=AX.X)
                v.tensor_mul(sqt[:, :], xt[i % 2][:, :], xt[i % 2][:, :])
                v.drain()
                v.reduce_sum(mcol[:, i:i + 1], sqt[:, :], axis=AX.X)
                v.drain()
                v.tensor_scalar_mul(ncol[:, i:i + 1], tcol[:, i:i + 1], 1.0 / D)
                v.drain()
                v.tensor_scalar_mul(tcol[:, i:i + 1], mcol[:, i:i + 1], 1.0 / D)
                v.drain()
                v.tensor_mul(mcol[:, i:i + 1], ncol[:, i:i + 1], ncol[:, i:i + 1])
                v.drain()
                v.tensor_sub(tcol[:, i:i + 1], tcol[:, i:i + 1], mcol[:, i:i + 1])
                v.drain()
                v.tensor_copy(mcol[:, i:i + 1], ncol[:, i:i + 1]).then_inc(vst, 1)
                v.wait_ge(ssq, i + 1)
                v.reciprocal(rcol[:, i:i + 1], tcol[:, i:i + 1])
                v.drain()
                v.tensor_mul(ncol[:, i:i + 1], mcol[:, i:i + 1], rcol[:, i:i + 1])
                v.drain()
                v.tensor_scalar_mul(ncol[:, i:i + 1], ncol[:, i:i + 1], -1.0).then_inc(vrs, 1)

        @blk.scalar
        def _(s):
            s.wait_ge(ldc.h, LDC_T)
            for i in range(NT):
                s.wait_ge(vst, i + 1)
                s.activation(tcol[:, i:i + 1], tcol[:, i:i + 1], AF.Sqrt,
                             bias=cepsv[:, :], scale=1.0).then_inc(ssq, 1)
                s.wait_ge(vrs, i + 1)
                s.activation(h1[:, i * D:(i + 1) * D], xt[i % 2][:, :], AF.Identity,
                             bias=ncol[:, i:i + 1], scale=rcol[:, i:i + 1]).then_inc(sh1, 1)

        @blk.tensor
        def _(t):
            for half in range(2):
                if half == 1:
                    t.wait_ge(v5, ct["v5"] + 1)
                for i in range(NT):
                    if half == 0:
                        t.wait_ge(sh1, i + 1)
                    for d1 in range(DC):
                        ins = t.matmul(pb[d1][:, :],
                                 h1[:, i * D + d1 * 128:i * D + (d1 + 1) * 128],
                                 h1[:, i * D + half * 512:i * D + half * 512 + 512],
                                 start=(i == 0), stop=(i == NT - 1))
                ins.then_inc(tn, 1)
                ct["tn"] += 1
        TN_GRAM0, TN_GRAM1 = ct["tn"] - 1, ct["tn"]

        @blk.vector
        def _(v):
            for half in range(2):
                v.wait_ge(tn, TN_GRAM0 + half)
                for d1 in range(DC):
                    ins = v.tensor_copy(
                        MG[:, d1 * D + half * 512:d1 * D + half * 512 + 512],
                        pb[d1][:, :])
                ins.then_inc(v5, 1)
                ct["v5"] += 1
        V5_GRAM = ct["v5"]

        @blk.sync
        def _(s):
            s.wait_ge(sh1, NT)
            n = 0
            for half in range(2):
                for dout in range(DC):
                    for k in range(DC):
                        if n >= 2:
                            s.wait_ge(tn, TN_GRAM1 + n - 1)
                        dma(s, ldw[n % 2], wqb[n % 2][:, :],
                            wq[k * 128:(k + 1) * 128, dout * 128:(dout + 1) * 128])
                        n += 1

        @blk.tensor
        def _(t):
            n = 0
            for half in range(2):
                t.wait_ge(v5, V5_GRAM - 1 + half + 1)
                for dout in range(DC):
                    for k in range(DC):
                        t.wait_ge(ldw[n % 2].h, 16 * (n // 2 + 1))
                        t.matmul(pb[dout][:, :], wqb[n % 2][:, :],
                                 MG[:, k * D + half * 512:k * D + half * 512 + 512],
                                 start=(k == 0), stop=(k == DC - 1)).then_inc(tn, 1)
                        ct["tn"] += 1
                        n += 1
        TN_P3 = ct["tn"]

        @blk.vector
        def _(v):
            for half in range(2):
                v.wait_ge(tn, TN_GRAM1 + 64 * (half + 1))
                for dout in range(DC):
                    ins = v.tensor_copy(
                        MG[:, dout * D + half * 512:dout * D + half * 512 + 512],
                        pb[dout][:, :])
                ins.then_inc(v5, 1)
                ct["v5"] += 1
        V5_P3 = ct["v5"]

        barrier()

        @blk.sync
        def _(s):
            for i in range(NT):
                if i >= 2:
                    s.wait_ge(vX2, i - 1)
                dma(s, ldx[i % 2], xt[i % 2][:, :],
                    x_in[i * 128:(i + 1) * 128, :])

        TN_P4 = ct["tn"]
        NSTO = 3 if stage == "attn" else 2

        p_tr = {}; p_attn = {}; p_h2t = {}; p_gate = {}
        _c = TN_P4
        p_tr[0] = _c; _c += 8
        for i in range(NT):
            p_attn[i] = _c; _c += 2
            if i + 1 < NT:
                p_tr[i + 1] = _c; _c += 8
            if i >= 1:
                p_h2t[i - 1] = _c; _c += 8
                p_gate[i - 1] = _c; _c += 1
        p_h2t[NT - 1] = _c; _c += 8
        p_gate[NT - 1] = _c; _c += 1
        TN_P4_END = _c

        def emit_tr(t, i):
            for d in range(DC):
                k = 8 * i + d
                if k >= 2:
                    t.wait_ge(evA, k - 1)
                t.transpose(pb[6 + (k % 2)][:, 0:128],
                            h1[:, i * D + d * 128:i * D + (d + 1) * 128],
                            cident[:, :]).then_inc(tn, 1)

        def emit_attn(t, i):
            t.wait_ge(evA, 8 * (i + 1))
            if i >= 1:
                t.wait_ge(vX2, i)
            for half in range(2):
                for d1 in range(DC):
                    ins = t.matmul(pb[half][:, :],
                             h1T[i % 2][:, d1 * 128:(d1 + 1) * 128],
                             MG[:, d1 * D + half * 512:d1 * D + half * 512 + 512],
                             start=(d1 == 0), stop=(d1 == DC - 1))
                ins.then_inc(tn, 1)

        def emit_h2t(t, i):
            t.wait_ge(sh1, NT + i + 1)
            for d in range(DC):
                k = 8 * i + d
                if k >= 2:
                    t.wait_ge(evH, k - 1)
                t.transpose(pb[3 + (k % 2)][:, 0:128],
                            h2t[:, d * 128:(d + 1) * 128],
                            cident[:, :]).then_inc(tn, 1)

        def emit_gate(t, i):
            t.wait_ge(evH, 8 * (i + 1))
            if i >= 1:
                t.wait_ge(vSm, i)
            for d in range(DC):
                t.matmul(pb[2][:, 0:E], h2T[:, d * 128:(d + 1) * 128],
                         sgwt[:, d * E:(d + 1) * E],
                         start=(d == 0), stop=False)
            t.matmul(pb[2][:, 0:E], conesr[:, :], sgb[:, :],
                     start=False, stop=True).then_inc(tn, 1)

        @blk.tensor
        def _(t):
            t.wait_ge(ldc.h, LDC_T)
            emit_tr(t, 0)
            for i in range(NT):
                emit_attn(t, i)
                if i + 1 < NT:
                    emit_tr(t, i + 1)
                if i >= 1:
                    emit_h2t(t, i - 1)
                    emit_gate(t, i - 1)
            emit_h2t(t, NT - 1)
            emit_gate(t, NT - 1)
        ct["tn"] = TN_P4_END

        @blk.vector
        def _(v):
            def evac_tr(v, i):
                for d in range(DC):
                    v.wait_ge(tn, p_tr[i] + d + 1)
                    if i >= 2 and d == 0:
                        v.wait_ge(tn, p_attn[i - 2] + 2)
                    v.tensor_copy(h1T[i % 2][:, d * 128:(d + 1) * 128],
                                  pb[6 + ((8 * i + d) % 2)][:, 0:128]).then_inc(evA, 1)

            def gate_soft(v, i):
                v.wait_ge(tn, p_gate[i] + 1)
                v.tensor_copy(lgt[:, 0:E], pb[2][:, 0:E])
                v.drain()
                v.max(lgt[:, 8:16], lgt[:, 0:E])
                v.drain()
                v.max_index(aidx[:, 0:8], lgt[:, 8:16], lgt[:, 0:E])
                v.drain()
                v.tensor_copy(idxl[:, i:i + 1], aidx[:, 0:1])
                v.tensor_scalar_mul(tcol[:, i:i + 1], lgt[:, 8:9], -1.0)
                v.drain()
                v.sem_inc(vSm, 1)
                v.wait_ge(sEx, i + 1)
                v.reduce_sum(tpl[:, i:i + 1], lgt[:, 0:E], axis=AX.X)
                v.drain()
                v.reciprocal(tpl[:, i:i + 1], tpl[:, i:i + 1])
                v.drain()

            evac_tr(v, 0)
            for i in range(NT):
                v.wait_ge(ldx[i % 2].h, 16 * (NT // 2 + i // 2 + 1))
                if i >= 1:
                    v.wait_ge(stx.h, 16 * i)
                for half in range(2):
                    v.wait_ge(tn, p_attn[i] + half + 1)
                    v.tensor_add(x2t[:, half * 512:half * 512 + 512],
                                 xt[i % 2][:, half * 512:half * 512 + 512],
                                 pb[half][:, :])
                v.drain()
                v.sem_inc(vX2, 1)
                if i + 1 < NT:
                    evac_tr(v, i + 1)
                v.reduce_sum(tcol[:, i:i + 1], x2t[:, :], axis=AX.X)
                v.tensor_mul(sqt[:, :], x2t[:, :], x2t[:, :])
                v.drain()
                v.reduce_sum(mcol[:, i:i + 1], sqt[:, :], axis=AX.X)
                v.drain()
                v.tensor_scalar_mul(ncol[:, i:i + 1], tcol[:, i:i + 1], 1.0 / D)
                v.drain()
                v.tensor_scalar_mul(tcol[:, i:i + 1], mcol[:, i:i + 1], 1.0 / D)
                v.drain()
                v.tensor_mul(mcol[:, i:i + 1], ncol[:, i:i + 1], ncol[:, i:i + 1])
                v.drain()
                v.tensor_sub(tcol[:, i:i + 1], tcol[:, i:i + 1], mcol[:, i:i + 1])
                v.drain()
                v.tensor_copy(mcol[:, i:i + 1], ncol[:, i:i + 1]).then_inc(vst, 1)
                v.wait_ge(ssq, NT + i + 1)
                v.reciprocal(rcol[:, i:i + 1], tcol[:, i:i + 1])
                v.drain()
                v.tensor_mul(ncol[:, i:i + 1], mcol[:, i:i + 1], rcol[:, i:i + 1])
                v.drain()
                v.tensor_scalar_mul(ncol[:, i:i + 1], ncol[:, i:i + 1], -1.0).then_inc(vrs, 1)
                if i >= 1:
                    gate_soft(v, i - 1)
            gate_soft(v, NT - 1)

        @blk.scalar
        def _(s):
            def h2evac(s, i):
                for d in range(DC):
                    s.wait_ge(tn, p_h2t[i] + d + 1)
                    s.activation(h2T[:, d * 128:(d + 1) * 128],
                                 pb[3 + ((8 * i + d) % 2)][:, 0:128], AF.Copy,
                                 bias=0.0, scale=1.0).then_inc(evH, 1)

            def expx(s, i):
                s.wait_ge(vSm, i + 1)
                s.activation(lgt[:, 0:E], lgt[:, 0:E], AF.Exp,
                             bias=tcol[:, i:i + 1], scale=1.0)
                s.drain()
                s.sem_inc(sEx, 1)

            for i in range(NT):
                s.wait_ge(vst, NT + i + 1)
                s.activation(tcol[:, i:i + 1], tcol[:, i:i + 1], AF.Sqrt,
                             bias=cepsv[:, :], scale=1.0).then_inc(ssq, 1)
                if i >= 1:
                    h2evac(s, i - 1)
                    expx(s, i - 1)
                s.wait_ge(vrs, NT + i + 1)
                if i >= 1:
                    s.wait_ge(tn, p_h2t[i - 1] + 8)
                    s.wait_ge(smh.h, 16 * i)
                s.activation(h2t[:, :], x2t[:, :], AF.Identity,
                             bias=ncol[:, i:i + 1], scale=rcol[:, i:i + 1]).then_inc(sh1, 1)
                s.drain()
                dma(s, stx, x2d[i * 128:(i + 1) * 128, :], x2t[:, :])
                if stage == "attn":
                    dma(s, sto, out[i * 128:(i + 1) * 128, :], x2t[:, :])
                dma(s, smh, h2shard[i * 128:(i + 1) * 128, :], h2t[:, :])
            h2evac(s, NT - 1)
            expx(s, NT - 1)

        @blk.gpsimd
        def _(g):
            for k in range(15):
                g.wait_ge(smh.h, 16 * 2 * (k + 1))
                g.collective_compute(
                    "AllGather", ALU.bypass,
                    replica_groups=[list(range(NCORES))],
                    ins=[h2shard[256 * k:256 * (k + 1), :].opt()],
                    outs=[h2glob[2048 * k:2048 * (k + 1), :].opt()],
                ).then_inc(cc, 1)
        ct["cc"] += 15

        if stage == "attn":
            @blk.scalar
            def _(s):
                s.wait_ge(sto.h, 16 * sto.n)
            return nc

        barrier()

        TN_F = ct["tn"]

        @blk.tensor
        def _(t):
            for j in range(16):
                if j >= 2:
                    t.wait_ge(evF, j - 1)
                src = idxl if j < 8 else tpl
                ph = j % 8
                t.matmul(pb[5 + (j % 2)][0:16, 0:NT],
                         cident[:, 16 * ph:16 * (ph + 1)], src[:, :],
                         start=True, stop=True).then_inc(tn, 1)
        ct["tn"] += 16

        @blk.vector
        def _(v):
            for j in range(16):
                dst = il16 if j < 8 else tp16
                ph = j % 8
                v.wait_ge(tn, TN_F + j + 1)
                v.tensor_copy(dst[0:16, ph:NCOLL:8],
                              pb[5 + (j % 2)][0:16, 0:NT]).then_inc(evF, 1)

        irepl = sqt[:, 0:NCOLL]
        onesl = sqt[:, NCOLL:2 * NCOLL]
        colsuml = sqt[:, 2 * NCOLL:3 * NCOLL]
        cumsl = sqt[:, 3 * NCOLL:4 * NCOLL]

        @blk.sync
        def _(s):
            s.wait_ge(evF, 8)
            for e in range(8):
                dma(s, ldr, irepl[16 * e:16 * (e + 1), :], il16[:, 0:NCOLL])
        LDR_IREPL = 16 * ldr.n

        @blk.vector
        def _(v):
            v.memset(onesl[:, :], 1.0)
            v.memset(il16[:, NCOLL:320], 0.0)
            v.wait_ge(ldr.h, LDR_IREPL)
            v.wait_ge(ldc.h, LDC_T)
            v.tensor_single_scalar(maskl[:, :], irepl[:, :], cecon[:, :], ALU.is_equal)
            v.drain()
            v.sem_inc(v5, 1)
            ct["v5"] += 1
        V5_MASKL = ct["v5"]

        @blk.tensor
        def _(t):
            t.wait_ge(v5, V5_MASKL)
            t.matmul(pb[4][0:128, 0:NCOLL], cbtriu[:, :], maskl[:, :],
                     start=True, stop=True).then_inc(t5, 1)
            ct["t5"] += 1
        T5_COLPL = ct["t5"]

        @blk.vector
        def _(v):
            v.wait_ge(t5, T5_COLPL)
            v.tensor_copy(colpl[:, :], pb[4][0:128, 0:NCOLL])
            v.drain()
            v.tensor_add(colsuml[:, :], colpl[:, :], maskl[:, :])
            v.drain()
            v.tensor_tensor_scan(cumsl[:, :], onesl[:, :], colsuml[:, :], 0.0,
                                 ALU.mult, ALU.add)
            v.drain()
            v.tensor_sub(rexl[:, :], cumsl[:, :], colsuml[:, :])
            v.drain()
            v.sem_inc(v5, 1)
            ct["v5"] += 1
        V5_SCANL = ct["v5"]

        @blk.tensor
        def _(t):
            t.wait_ge(v5, V5_SCANL)
            t.matmul(pb[4][0:128, 0:NCOLL], cgspread[:, :], rexl[:, :],
                     start=True, stop=True)
            t.matmul(pb[5][0:8, 0:1], csel15[:, :], cumsl[:, NCOLL - 1:NCOLL],
                     start=True, stop=True).then_inc(t5, 1)
            ct["t5"] += 1
        T5_SPREADL = ct["t5"]

        @blk.vector
        def _(v):
            v.wait_ge(t5, T5_SPREADL)
            v.tensor_copy(rexl[:, :], pb[4][0:128, 0:NCOLL])
            v.tensor_copy(cnts[:, :], pb[5][0:8, 0:1])
            v.drain()
            v.tensor_copy(il16[0:8, NCOLL:NCOLL + 1], cnts[:, :])
            v.drain()
            v.sem_inc(v5, 1)
            ct["v5"] += 1
        V5_CNT = ct["v5"]

        @blk.scalar
        def _(s):
            s.wait_ge(v5, V5_CNT)
            dma(s, sid,
                h2shard[TL:TL + 5, :].rearrange("a b -> (a b)")
                .rearrange("(q s) -> q s", s=320),
                il16[:, :])

        @blk.gpsimd
        def _(g):
            g.wait_ge(sid.h, 16 * sid.n)
            g.collective_compute(
                "AllGather", ALU.bypass,
                replica_groups=[list(range(NCORES))],
                ins=[h2shard[TL:TL + 5, :].opt()],
                outs=[idxg2.opt()],
            ).then_inc(cc, 1)
            g.wait_ge(smh.h, 16 * 8 * 4)
            g.collective_compute(
                "AllGather", ALU.bypass,
                replica_groups=[list(range(NCORES))],
                ins=[h2shard[3840:TL, :].opt()],
                outs=[h2glob[15 * 2048:16 * 2048, :].opt()],
            ).then_inc(cc, 1)
        ct["cc"] += 2

        barrier()
        regA.close()

        if stage == "p4b":
            @blk.scalar
            def _(s):
                dma(s, sto, out[0:16, 0:NCOLL], il16[:, :])
                dma(s, sto, out[16:32, 0:NCOLL], tp16[:, :])
                s.wait_ge(sto.h, 16 * sto.n)
            return nc

        regB = contextlib.ExitStack()
        es.enter_context(regB)
        rbt = lambda name, shape, dtt=F32: regB.enter_context(
            nc.sbuf_tensor(name, shape, dtt))
        irep = rbt("irep", [128, NCOL])
        maskg = rbt("maskg", [128, NCOL])
        colp = rbt("colp", [128, NCOL])
        colsum = rbt("colsum", [128, NCOL])
        cums = rbt("cums", [128, NCOL])
        rex = rbt("rex", [128, NCOL])
        grank = rbt("grank", [128, NCOL])
        onesw = rbt("onesw", [128, NCOL])
        fullsel = rbt("fullsel", [128, NCOL])
        i16f = rbt("i16f", [16, NCOL], I16)
        i16rep = rbt("i16rep", [128, NCOL], I16)
        ctrashr = rbt("ctrashr", [128, NCOL])
        idw = rbt("idw", [128, NST * 64])
        idwb = [rbt(f"idwb{k}", [128, NST * 64]) for k in range(2)]
        cgid = rbt("cgid", [128, 256])
        grankl = rbt("grankl", [128, NCOLL])
        kml = rbt("kml", [128, NCOLL])
        pcb = rbt("pcb", [128, 1])
        ctab = rbt("ctab", [8, 8])
        t16a = rbt("t16a", [16, NCOLL])
        t16a2 = rbt("t16a2", [16, NCOLL])
        t16b = rbt("t16b", [16, NCOLL])
        zt = rbt("zt", [128, D])

        CC_IDX = 16
        CC_H2 = 17

        idxt3 = (idxg2.rearrange("a b -> (a b)")
                 .rearrange("(c z) -> c z", c=NCORES)
                 .rearrange("c (q s) -> c q s", s=320))

        V5_0 = ct["v5"]

        @blk.sync
        def _(s):
            dma(s, ldr, ctrashr[:, :], C["ctrash"])
            dma(s, ldr, cgid[:, :], C["gid"])
            s.wait_ge(cc, CC_IDX)
            for e in range(8):
                dma(s, ldr,
                    irep[16 * e:16 * (e + 1), :].rearrange("q (c s) -> q c s", c=8),
                    idxt3.rearrange("c q s -> q c s")[:, :, 0:NCOLL])
            s.dma_start(
                out=ctab[:, :],
                in_=idxt3[:, 0:8, NCOLL:NCOLL + 1].rearrange("c q s -> c (q s)"),
                allow_slow_non_contiguous=True).then_inc(ldr.h, 16)
            ldr.n += 1
        LDR_IREP = 16 * ldr.n

        @blk.vector
        def _(v):
            v.memset(onesw[:, :], 1.0)
            v.memset(zt[:, :], 0.0)
            v.drain()
            v.sem_inc(v5, 1)
            ct["v5"] += 1
            v.wait_ge(ldr.h, LDR_IREP)
            v.tensor_single_scalar(maskg[:, :], irep[:, :], cecon[:, :], ALU.is_equal)
            v.drain()
            v.sem_inc(v5, 1)
            ct["v5"] += 1
        V5_MASK = ct["v5"]

        @blk.tensor
        def _(t):
            t.wait_ge(v5, V5_MASK)
            for j in range(3):
                t.matmul(pb[j][:, :], cbtriu[:, :], maskg[:, j * 512:(j + 1) * 512],
                         start=True, stop=True)
            t.matmul(pb[3][:, :], cbtriu[:, :], maskg[:, 3 * 512:4 * 512],
                     start=True, stop=True).then_inc(t5, 1)
            ct["t5"] += 1
        T5_COLP = ct["t5"]

        @blk.vector
        def _(v):
            v.wait_ge(t5, T5_COLP)
            for j in range(4):
                v.tensor_copy(colp[:, j * 512:(j + 1) * 512], pb[j][:, :])
            v.drain()
            v.tensor_add(colsum[:, :], colp[:, :], maskg[:, :])
            v.drain()
            v.tensor_tensor_scan(cums[:, :], onesw[:, :], colsum[:, :], 0.0,
                                 ALU.mult, ALU.add)
            v.drain()
            v.tensor_sub(rex[:, :], cums[:, :], colsum[:, :])
            v.drain()
            v.sem_inc(v5, 1)
            ct["v5"] += 1
        V5_SCAN = ct["v5"]

        @blk.tensor
        def _(t):
            t.wait_ge(v5, V5_SCAN)
            for j in range(3):
                t.matmul(pb[j][:, :], cgspread[:, :], rex[:, j * 512:(j + 1) * 512],
                         start=True, stop=True)
            t.matmul(pb[3][:, :], cgspread[:, :], rex[:, 3 * 512:4 * 512],
                     start=True, stop=True).then_inc(t5, 1)
            ct["t5"] += 1
        T5_SPREAD = ct["t5"]

        @blk.vector
        def _(v):
            v.wait_ge(t5, T5_SPREAD)
            for j in range(4):
                v.tensor_copy(rex[:, j * 512:(j + 1) * 512], pb[j][:, :])
            v.drain()
            v.tensor_add(grank[:, :], colp[:, :], rex[:, :])
            v.drain()
            v.tensor_single_scalar(fullsel[:, :], grank[:, :], float(CAP), ALU.is_lt)
            v.drain()
            v.tensor_mul(fullsel[:, :], fullsel[:, :], maskg[:, :])
            v.drain()
            v.tensor_sub(cums[:, :], grank[:, :], ctrashr[:, :])
            v.drain()
            v.tensor_mul(fullsel[:, :], fullsel[:, :], cums[:, :])
            v.drain()
            v.tensor_add(fullsel[:, :], fullsel[:, :], ctrashr[:, :])
            v.drain()
            v.sem_inc(v5, 1)
            ct["v5"] += 1
        V5_FSEL = ct["v5"]

        @blk.tensor
        def _(t):
            t.wait_ge(v5, V5_FSEL)
            for j in range(3):
                t.matmul(pb[j][0:16, :], cgsel[:, :], fullsel[:, j * 512:(j + 1) * 512],
                         start=True, stop=True)
            t.matmul(pb[3][0:16, :], cgsel[:, :], fullsel[:, 3 * 512:4 * 512],
                     start=True, stop=True).then_inc(t5, 1)
            ct["t5"] += 1
        T5_GSEL = ct["t5"]

        @blk.vector
        def _(v):
            v.wait_ge(t5, T5_GSEL)
            for j in range(4):
                v.tensor_copy(i16f[:, j * 512:(j + 1) * 512], pb[j][0:16, :])
            v.drain()
            v.sem_inc(v5, 1)
            ct["v5"] += 1
        V5_I16F = ct["v5"]

        @blk.tensor
        def _(t):
            t.wait_ge(ldr.h, LDR_IREP)
            t.matmul(pb[5][0:8, 0:1], ctab[:, :], ccorelt[:, :],
                     start=True, stop=True).then_inc(t5, 1)
            ct["t5"] += 1
        T5_PC1 = ct["t5"]

        @blk.vector
        def _(v):
            v.wait_ge(t5, T5_PC1)
            v.tensor_copy(cnts[:, :], pb[5][0:8, 0:1])
            v.drain()
            v.sem_inc(v5, 1)
            ct["v5"] += 1
        V5_PC1 = ct["v5"]

        @blk.tensor
        def _(t):
            t.wait_ge(v5, V5_PC1)
            t.matmul(pb[5][:, 0:1], cgrp8[:, :], cnts[:, :],
                     start=True, stop=True).then_inc(t5, 1)
            ct["t5"] += 1
        T5_PC2 = ct["t5"]

        @blk.vector
        def _(v):
            v.wait_ge(t5, T5_PC2)
            v.tensor_copy(pcb[:, :], pb[5][:, 0:1])
            v.drain()
            v.tensor_add(grankl[:, :], colpl[:, :], rexl[:, :])
            v.drain()
            v.tensor_scalar_add(grankl[:, :], grankl[:, :], pcb[:, :])
            v.drain()
            v.tensor_single_scalar(kml[:, :], grankl[:, :], float(CAP), ALU.is_lt)
            v.drain()
            v.tensor_mul(kml[:, :], kml[:, :], maskl[:, :])
            v.drain()
            v.tensor_mul(grankl[:, :], grankl[:, :], maskl[:, :])
            v.drain()
            v.sem_inc(v5, 1)
            ct["v5"] += 1
        V5_OWN = ct["v5"]

        @blk.tensor
        def _(t):
            t.wait_ge(v5, V5_OWN)
            t.matmul(pb[4][0:16, 0:NCOLL], cgsum16[:, :], kml[:, :],
                     start=True, stop=True)
            t.matmul(pb[5][0:16, 0:NCOLL], cgsum16[:, :], grankl[:, :],
                     start=True, stop=True)
            t.matmul(pb[6][0:16, 0:NCOLL], cgsumE[:, :], maskl[:, :],
                     start=True, stop=True).then_inc(t5, 1)
            ct["t5"] += 1
        T5_SEL = ct["t5"]

        @blk.vector
        def _(v):
            v.wait_ge(t5, T5_SEL)
            v.tensor_copy(t16a[:, :], pb[4][0:16, 0:NCOLL])
            v.tensor_copy(t16b[:, :], pb[5][0:16, 0:NCOLL])
            v.tensor_copy(g16[:, :], pb[6][0:16, 0:NCOLL])
            v.drain()
            v.tensor_single_scalar(s16[:, :], t16b[:, :], 640.0, ALU.is_ge)
            for m in range(2, 8):
                v.drain()
                v.tensor_single_scalar(t16a2[:, :], t16b[:, :], 640.0 * m, ALU.is_ge)
                v.drain()
                v.tensor_add(s16[:, :], s16[:, :], t16a2[:, :])
            v.drain()
            v.tensor_single_scalar(t16a2[:, :], t16b[:, :], 4480.0, ALU.is_ge)
            v.drain()
            v.tensor_single_scalar(il16[0:16, 0:NCOLL], t16b[:, :], 4864.0, ALU.is_ge)
            v.drain()
            v.scalar_tensor_tensor(t16b[:, :], s16[:, :], 4480.0, t16b[:, :],
                                   ALU.mult, ALU.add)
            v.drain()
            v.scalar_tensor_tensor(t16b[:, :], il16[0:16, 0:NCOLL], 2688.0,
                                   t16b[:, :], ALU.mult, ALU.add)
            v.drain()
            v.tensor_single_scalar(s16[:, :], t16a2[:, :], -256.0, ALU.mult)
            v.drain()
            v.tensor_single_scalar(s16[:, :], s16[:, :], 640.0, ALU.add)
            v.drain()
            v.scalar_tensor_tensor(s16[:, :], il16[0:16, 0:NCOLL], -128.0,
                                   s16[:, :], ALU.mult, ALU.add)
            v.drain()
            v.tensor_scalar_mul(g16[:, :], g16[:, :], 1.0 / 5120.0)
            v.drain()
            v.tensor_mul(g16[:, :], g16[:, :], s16[:, :])
            v.drain()
            v.tensor_add(g16[:, :], g16[:, :], t16b[:, :])
            v.drain()
            v.tensor_mul(g16[:, :], g16[:, :], t16a[:, :])
            v.tensor_mul(s16[:, :], tp16[:, :], t16a[:, :])
            v.drain()
            v.sem_inc(v5, 1)
            ct["v5"] += 1
        V5_G16 = ct["v5"]

        @blk.tensor
        def _(t):
            t.wait_ge(v5, V5_G16)
            for j, src in enumerate([g16, s16]):
                for sh in range(2):
                    ins = t.transpose(pb[4 + 2 * j + sh][0:128, 0:16],
                                src[:, sh * 128:(sh + 1) * 128],
                                cident[0:16, 0:16])
            ins.then_inc(t5, 1)
            ct["t5"] += 1
        T5_GT = ct["t5"]

        @blk.vector
        def _(v):
            v.wait_ge(t5, T5_GT)
            for sh in range(2):
                v.tensor_copy(gcols[:, sh * 16:(sh + 1) * 16], pb[4 + sh][0:128, 0:16])
                v.tensor_copy(scols[:, sh * 16:(sh + 1) * 16], pb[6 + sh][0:128, 0:16])
            v.drain()
            v.sem_inc(v5, 1)
            ct["v5"] += 1

        if stage == "routemath":
            @blk.scalar
            def _(s):
                s.wait_ge(v5, ct["v5"])
                dma(s, sto, out[0:128, :], fullsel[:, 0:1024])
                dma(s, sto, out[128:256, :], fullsel[:, 1024:2048])
                dma(s, sto, out[256:384, 0:NCOLL], grankl[:, :])
                dma(s, sto, out[384:400, 0:NCOLL], g16[:, :])
                dma(s, sto, out[400:416, 0:NCOLL], s16[:, :])
                dma(s, sto, out[416:544, 0:32], gcols.bitcast(F32)[:, :])
                dma(s, sto, out[544:672, 0:32], scols[:, :])
                dma(s, sto, out[672:688, 0:1024], i16f.bitcast(F32)[:, :])
                s.wait_ge(sto.h, 16 * sto.n)
            return nc

        @blk.sync
        def _(s):
            s.wait_ge(v5, V5_I16F)
            for e in range(8):
                dma(s, ldz, i16rep[16 * e:16 * (e + 1), :], i16f[:, :])
        LDZ_T = 16 * ldz.n

        @blk.gpsimd
        def _(g):
            g.wait_ge(ldz.h, LDZ_T)
            g.wait_ge(qpz.h, 16 * qpz.n)
            g.wait_ge(ldr.h, LDR_IREP)
            for k in range(8):
                g.dma_scatter_add(
                    idtab[k * (CAP + 1024):(k + 1) * (CAP + 1024), 0:1],
                    cgid[:, 32 * k:32 * (k + 1)].rearrange("p (n one) -> p n one", one=1),
                    i16rep[:, 256 * k:256 * (k + 1)],
                    num_idxs=TG // 8, num_idxs_reg=TG // 8, elem_size=1, elem_step=64,
                ).then_inc(qsc[k].h, 16)
                qsc[k].n += 1

        V5_IDW0 = ct["v5"]

        @blk.scalar
        def _(s):
            for k in range(8):
                s.wait_ge(qsc[k].h, 16 * qsc[k].n)
                if k >= 2:
                    s.wait_ge(v5, V5_IDW0 + k - 1)
                dma(s, qid[k % 2],
                    idwb[k % 2][:, :].rearrange("p (a c) -> p a c", c=64),
                    idtab[k * (CAP + 1024):(k + 1) * (CAP + 1024), :]
                    .rearrange("(a p) c -> p a c", p=128)[:, 0:NST, :])

        @blk.vector
        def _(v):
            for k in range(8):
                v.wait_ge(qid[k % 2].h, 16 * (k // 2 + 1))
                if k == 0:
                    v.tensor_copy(idw[:, 0:NST * 64:64],
                                  idwb[0][:, 0:NST * 64:64]).then_inc(v5, 1)
                else:
                    v.tensor_add(idw[:, 0:NST * 64:64], idw[:, 0:NST * 64:64],
                                 idwb[k % 2][:, 0:NST * 64:64]).then_inc(v5, 1)
                v.drain()
                ct["v5"] += 1
        V5_IDW8 = ct["v5"]

        @blk.vector
        def _(v):
            v.wait_ge(v5, V5_IDW8)
            v.tensor_single_scalar(zt[:, 0:NST], idw[:, 0:NST * 64:64], 1.0, ALU.subtract)
            v.drain()
            v.tensor_single_scalar(zt[:, 0:NST], zt[:, 0:NST], 0.0, ALU.max)
            v.drain()
            v.tensor_single_scalar(zt[:, NST:2 * NST], zt[:, 0:NST], 4096.0, ALU.is_ge)
            for m in range(2, 8):
                v.drain()
                v.tensor_single_scalar(zt[:, 3 * NST:4 * NST], zt[:, 0:NST],
                                       4096.0 * m, ALU.is_ge)
                v.drain()
                v.tensor_add(zt[:, NST:2 * NST], zt[:, NST:2 * NST],
                             zt[:, 3 * NST:4 * NST])
            v.drain()
            v.scalar_tensor_tensor(zt[:, 2 * NST:3 * NST], zt[:, NST:2 * NST],
                                   -4096.0, zt[:, 0:NST], ALU.mult, ALU.add)
            v.drain()
            v.tensor_single_scalar(zt[:, 0:NST], zt[:, 2 * NST:3 * NST], 256.0, ALU.is_ge)
            for m in range(2, 16):
                v.drain()
                v.tensor_single_scalar(zt[:, 3 * NST:4 * NST], zt[:, 2 * NST:3 * NST],
                                       256.0 * m, ALU.is_ge)
                v.drain()
                v.tensor_add(zt[:, 0:NST], zt[:, 0:NST], zt[:, 3 * NST:4 * NST])
            v.drain()
            v.scalar_tensor_tensor(zt[:, 2 * NST:3 * NST], zt[:, 0:NST],
                                   1792.0, zt[:, 2 * NST:3 * NST], ALU.mult, ALU.add)
            v.drain()
            v.scalar_tensor_tensor(zt[:, 2 * NST:3 * NST], zt[:, NST:2 * NST],
                                   256.0, zt[:, 2 * NST:3 * NST], ALU.mult, ALU.add)
            v.drain()
            v.tensor_copy(idcols[:, :], zt[:, 2 * NST:3 * NST])
            v.drain()
            v.sem_inc(v5, 1)
            ct["v5"] += 1
        V5_IDC = ct["v5"]

        barrier()

        regB.close()

        regC = contextlib.ExitStack()
        es.enter_context(regC)
        rct = lambda name, shape, dtt=F32: regC.enter_context(
            nc.sbuf_tensor(name, shape, dtt))
        w1sb = rct("w1sb", [128, DC * FBS])
        w2sb = rct("w2sb", [128, FCB * D])
        y1sb = rct("y1sb", [128, FCB * TSL])
        sfb = [rct(f"sfb{k}", [128, DC * TSL]) for k in range(2)]
        y2ev = [rct(f"y2ev{k}", [128, D]) for k in range(2)]
        y2pl = [rct(f"y2pl{k}", [128, D]) for k in range(2)]
        regP6 = contextlib.ExitStack()
        es.enter_context(regP6)
        slt = [regP6.enter_context(nc.sbuf_tensor(f"slt{k}", [128, D], F32))
               for k in range(2)]
        sle = [regP6.enter_context(nc.sbuf_tensor(f"sle{k}", [128, D], F32))
               for k in range(2)]

        TN_P6 = ct["tn"]

        @blk.gpsimd
        def _(g):
            g.wait_ge(cc, CC_H2)
            g.wait_ge(v5, V5_IDC)
            for n in range(NST):
                if n >= 2:
                    g.wait_ge(tn, TN_P6 + 8 * (n - 1))
                g.indirect_dma_start(
                    out=slt[n % 2][:, :],
                    out_offset=None,
                    in_=h2glob[:, :],
                    in_offset=IndirectOffsetOnAxis(ap=idcols[:, n:n + 1], axis=0),
                ).then_inc(qsl[n % 2].h, 16)
                qsl[n % 2].n += 1

        @blk.sync
        def _(s):
            for dcx in range(DC):
                dma(s, ldwb, w1sb[:, dcx * FBS:(dcx + 1) * FBS],
                    w1[dcx * 128:(dcx + 1) * 128, 0:FBS])
            for fc in range(FCB):
                dma(s, ldwb, w2sb[:, fc * D:(fc + 1) * D],
                    w2[fc * 128:(fc + 1) * 128, :])
        LDWB_0 = 16 * ldwb.n

        @blk.tensor
        def _(t):
            for n in range(NST):
                t.wait_ge(qsl[n % 2].h, 16 * (n // 2 + 1))
                for d in range(DC):
                    k = 8 * n + d
                    if k >= 2:
                        t.wait_ge(evB0 if k % 2 == 0 else evB1, k // 2)
                    t.transpose(pb[k % 2][:, 0:128],
                                slt[n % 2][:, d * 128:(d + 1) * 128],
                                cident[:, :]).then_inc(tn, 1)
        ct["tn"] += 8 * NST

        @blk.vector
        def _(v):
            for n in range(NST):
                for d in range(0, DC, 2):
                    k = 8 * n + d
                    v.wait_ge(tn, TN_P6 + k + 1)
                    if n >= 2 and d == 0:
                        v.wait_ge(stf[n % 2].h, 16 * 8 * (n // 2))
                    v.tensor_copy(sle[n % 2][:, d * 128:(d + 1) * 128],
                                  pb[k % 2][:, 0:128]).then_inc(evB0, 1)

        @blk.scalar
        def _(s):
            for n in range(NST):
                for d in range(1, DC, 2):
                    k = 8 * n + d
                    s.wait_ge(tn, TN_P6 + k + 1)
                    if n >= 2 and d == 1:
                        s.wait_ge(stf[n % 2].h, 16 * 8 * (n // 2))
                    s.activation(sle[n % 2][:, d * 128:(d + 1) * 128],
                                 pb[k % 2][:, 0:128], AF.Copy,
                                 bias=0.0, scale=1.0).then_inc(evB1, 1)
                s.wait_ge(evB0, 4 * (n + 1))
                s.wait_ge(evB1, 4 * (n + 1))
                for d in range(DC):
                    dma(s, stf[n % 2],
                        slabf[d * 128:(d + 1) * 128, n * 128:(n + 1) * 128],
                        sle[n % 2][:, d * 128:(d + 1) * 128])

        for fb in range(FBLK):
            if fb == 1:
                @blk.sync
                def _(s):
                    for dcx in range(DC):
                        dma(s, ldwb, w1sb[:, dcx * FBS:(dcx + 1) * FBS],
                            w1[dcx * 128:(dcx + 1) * 128, FBS:FF])
                    for fc in range(FCB):
                        dma(s, ldwb, w2sb[:, fc * D:(fc + 1) * D],
                            w2[FBS + fc * 128:FBS + (fc + 1) * 128, :])

            LDWB_FB = 16 * ldwb.n
            TN_FB = ct["tn"]

            @blk.sync
            def _(s):
                for ts_ in range(NSL):
                    if ts_ >= 2:
                        s.wait_ge(tn, TN_FB + 20 * (ts_ - 1))
                    if fb == 0:
                        ntile = 2 * ts_ + 2
                        s.wait_ge(stf[0].h, 16 * 8 * ((ntile + 1) // 2))
                        s.wait_ge(stf[1].h, 16 * 8 * (ntile // 2))
                    for d in range(DC):
                        dma(s, lsf[ts_ % 2], sfb[ts_ % 2][:, d * TSL:(d + 1) * TSL],
                            slabf[d * 128:(d + 1) * 128, ts_ * TSL:(ts_ + 1) * TSL])
                    if fb == 1:
                        for tc in range(2):
                            if ts_ >= 1:
                                s.wait_ge(evD, (NSL + ts_ - 1) * 4 + 2 * tc + 2)
                            dma(s, lyp[tc % 2], y2pl[tc % 2][:, :],
                                y2partial[ts_ * TSL + tc * 128:
                                          ts_ * TSL + (tc + 1) * 128, :])

            @blk.tensor
            def _(t):
                t.wait_ge(ldwb.h, LDWB_FB)
                for ts_ in range(NSL):
                    gs = fb * NSL + ts_
                    t.wait_ge(lsf[ts_ % 2].h, 16 * DC * (gs // 2 + 1))
                    for fc in range(FCB):
                        bank = 2 + (fc % 2)
                        gfc = (fb * NSL + ts_) * FCB + fc
                        if gfc >= 2:
                            t.wait_ge(evC, gfc - 1)
                        for dcx in range(DC):
                            ins = t.matmul(pb[bank][:, 0:TSL],
                                     w1sb[:, dcx * FBS + fc * 128:dcx * FBS + (fc + 1) * 128],
                                     sfb[ts_ % 2][:, dcx * TSL:(dcx + 1) * TSL],
                                     start=(dcx == 0), stop=(dcx == DC - 1))
                        ins.then_inc(tn, 1)
                    t.wait_ge(evC, (fb * NSL + ts_) * FCB + FCB)
                    for tc in range(2):
                        for dh in range(2):
                            bank = 4 + (2 * tc + dh) % 2
                            gi = (fb * NSL + ts_) * 4 + 2 * tc + dh
                            if gi >= 2:
                                t.wait_ge(evD, gi - 1)
                            for fc in range(FCB):
                                ins = t.matmul(pb[bank][:, :],
                                         y1sb[:, fc * TSL + tc * 128:fc * TSL + (tc + 1) * 128],
                                         w2sb[:, fc * D + dh * 512:fc * D + dh * 512 + 512],
                                         start=(fc == 0),
                                         stop=(fb == 1 and fc == FCB - 1))
                            if fb == 0:
                                ins = t.matmul(pb[bank][:, :], conesr[:, :],
                                         sb2r[:, dh * 512:dh * 512 + 512],
                                         start=False, stop=True)
                            ins.then_inc(tn, 1)
            ct["tn"] += 20 * NSL

            @blk.scalar
            def _(s):
                tgt = y2partial if fb == 0 else y2slab
                for ts_ in range(NSL):
                    for fc in range(FCB):
                        s.wait_ge(tn, TN_FB + 20 * ts_ + fc + 1)
                        s.activation(y1sb[:, fc * TSL:(fc + 1) * TSL],
                                     pb[2 + (fc % 2)][:, 0:TSL], AF.Relu,
                                     bias=sb1t[:, fb * FCB + fc:fb * FCB + fc + 1],
                                     scale=1.0).then_inc(evC, 1)
                    for tc in range(2):
                        s.wait_ge(evD, (fb * NSL + ts_) * 4 + 2 * tc + 2)
                        dma(s, sty[tc % 2],
                            tgt[ts_ * TSL + tc * 128:ts_ * TSL + (tc + 1) * 128, :],
                            y2ev[tc % 2][:, :])

            @blk.vector
            def _(v):
                for ts_ in range(NSL):
                    gs = fb * NSL + ts_
                    for tc in range(2):
                        if gs >= 1:
                            v.wait_ge(sty[tc % 2].h, 16 * gs)
                        for dh in range(2):
                            v.wait_ge(tn, TN_FB + 20 * ts_ + FCB + 2 * tc + dh + 1)
                            bank = 4 + (2 * tc + dh) % 2
                            if fb == 0:
                                v.tensor_copy(y2ev[tc % 2][:, dh * 512:dh * 512 + 512],
                                              pb[bank][:, :]).then_inc(evD, 1)
                            else:
                                v.wait_ge(lyp[tc % 2].h, 16 * (ts_ + 1))
                                v.tensor_add(y2ev[tc % 2][:, dh * 512:dh * 512 + 512],
                                             y2pl[tc % 2][:, dh * 512:dh * 512 + 512],
                                             pb[bank][:, :]).then_inc(evD, 1)

            if fb == 1:
                YB = [0, 640, 1280, 1920, 2560, 3200, 3840, 4480, 4864, 5120]

                @blk.gpsimd
                def _(g):
                    for k in range(9):
                        nslice = (YB[k + 1] + TSL - 1) // TSL
                        g.wait_ge(sty[0].h, 16 * (NSL + nslice))
                        g.wait_ge(sty[1].h, 16 * (NSL + nslice))
                        g.collective_compute(
                            "AllGather", ALU.bypass,
                            replica_groups=[list(range(NCORES))],
                            ins=[y2slab[YB[k]:YB[k + 1], :].opt()],
                            outs=[y2glob[8 * YB[k]:8 * YB[k + 1], :].opt()],
                        ).then_inc(cc, 1)
                ct["cc"] += 9
                CC_Y2 = ct["cc"]

            barrier()

        regP6.close()
        regC.close()


        regD = contextlib.ExitStack()
        es.enter_context(regD)
        gout = [regD.enter_context(nc.sbuf_tensor(f"gout{k}", [128, D], F32))
                for k in range(4)]
        x2b = [regD.enter_context(nc.sbuf_tensor(f"x2b{k}", [128, D], F32))
               for k in range(4)]
        otb = [regD.enter_context(nc.sbuf_tensor(f"otb{k}", [128, D], F32))
               for k in range(4)]

        x2v = x2d.rearrange("(r s) d -> r s d", s=16)
        outv = out.rearrange("(r s) d -> r s d", s=16)

        @blk.sync
        def _(s):
            for gi in range(32):
                if gi >= 4:
                    s.wait_ge(vO, gi - 3)
                q, sh = gi % 16, gi // 16
                dma(s, lx4[gi % 4], x2b[gi % 4][:, :],
                    x2v[sh * 128:(sh + 1) * 128, q, :])

        @blk.gpsimd
        def _(g):
            g.wait_ge(cc, CC_Y2)
            for gi in range(32):
                if gi >= 4:
                    g.wait_ge(vO, gi - 3)
                q, sh = gi % 16, gi // 16
                g.indirect_dma_start(
                    out=gout[gi % 4][:, :],
                    out_offset=None,
                    in_=y2glob[:, :],
                    in_offset=IndirectOffsetOnAxis(
                        ap=gcols[:, sh * 16 + q:sh * 16 + q + 1], axis=0),
                ).then_inc(qg4[gi % 4].h, 16)
                qg4[gi % 4].n += 1

        @blk.vector
        def _(v):
            for gi in range(32):
                q, sh = gi % 16, gi // 16
                v.wait_ge(lx4[gi % 4].h, 16 * (gi // 4 + 1))
                v.wait_ge(qg4[gi % 4].h, 16 * (gi // 4 + 1))
                if gi >= 4:
                    v.wait_ge(sp4[gi % 4].h, 16 * (gi // 4))
                v.scalar_tensor_tensor(otb[gi % 4][:, :], gout[gi % 4][:, :],
                                       scols[:, sh * 16 + q:sh * 16 + q + 1],
                                       x2b[gi % 4][:, :],
                                       ALU.mult, ALU.add).then_inc(vO, 1)

        @blk.scalar
        def _(s):
            for gi in range(32):
                q, sh = gi % 16, gi // 16
                s.wait_ge(vO, gi + 1)
                dma(s, sp4[gi % 4], outv[sh * 128:(sh + 1) * 128, q, :],
                    otb[gi % 4][:, :])
            for k in range(4):
                s.wait_ge(sp4[k].h, 16 * sp4[k].n)

    return nc


_NC_CACHE = {}


def _get_nc():
    if _STAGE not in _NC_CACHE:
        nc = _build(_STAGE)
        nc.finalize()
        _NC_CACHE[_STAGE] = nc
    return _NC_CACHE[_STAGE]


def kernel(L0_attn_hidden_states, L0_attn_W_q, L0_ffn_gate_w, L0_ffn_gate_b,
           L0_ffn_w1, L0_ffn_b1, L0_ffn_w2, L0_ffn_b2):
    x = np.asarray(L0_attn_hidden_states, dtype=np.float32)
    wqv = np.ascontiguousarray(np.asarray(L0_attn_W_q, dtype=np.float32))
    gw = np.asarray(L0_ffn_gate_w, dtype=np.float32)
    gbv = np.asarray(L0_ffn_gate_b, dtype=np.float32)
    w1v = np.asarray(L0_ffn_w1, dtype=np.float32)
    b1v = np.asarray(L0_ffn_b1, dtype=np.float32)
    w2v = np.asarray(L0_ffn_w2, dtype=np.float32)
    b2v = np.asarray(L0_ffn_b2, dtype=np.float32)

    consts = _host_consts()
    gwtv = np.ascontiguousarray(gw.T)
    gbr = np.ascontiguousarray(gbv.reshape(1, E))
    in_maps = []
    for c in range(NCORES):
        gsel = np.zeros((128, 16), np.float32)
        for q in range(16):
            gsel[16 * c + q, q] = 1.0
        clt = np.zeros((8, 1), np.float32)
        clt[:c, 0] = 1.0
        m = {
            "x": np.ascontiguousarray(x[c]),
            "wq": wqv,
            "gwt": gwtv,
            "gb": gbr,
            "w1": np.ascontiguousarray(w1v[c]),
            "b1t": np.ascontiguousarray(b1v[c].reshape(FF // 128, 128).T),
            "w2": np.ascontiguousarray(w2v[c]),
            "b2r": np.ascontiguousarray(b2v[c].reshape(1, D)),
            "gsel": gsel,
            "corelt": clt,
        }
        m.update(consts)
        in_maps.append(m)

    nc = _get_nc()
    res = run_bass_kernel_spmd(nc, in_maps, core_ids=list(range(NCORES)))
    outp = np.stack([np.asarray(r["out"]) for r in res.results], axis=0)
    return outp.astype(np.float32)
